# version 16
# baseline (speedup 1.0000x reference)
import numpy as np
import ml_dtypes

import concourse.bass as bass
import concourse.mybir as mybir
from concourse.bass_utils import run_bass_kernel_spmd

F32 = mybir.dt.float32
BF16 = mybir.dt.bfloat16
I32 = mybir.dt.int32
AX = mybir.AxisListType.X
OP = mybir.AluOpType
AF = mybir.ActivationFunctionType

NCORES = 8
S = 8192
DM = 2048
T = S // NCORES
H = 16
HD = 128
HPC = H // NCORES
FF = 8192
R = 16
CH = S // 128
TT = T // 128
FC = DM // 128
NG = S // 512
EPS_LN = 1e-5
EPS_A = 1e-6
EPS_RMS = 1e-6
MOD_SCALE = 0.1
SCALE_B = float(1.0 / np.sqrt(128.0))

ENGINES = ("tensor", "vector", "scalar", "gpsimd", "sync")


class Sem:
    __slots__ = ("h", "count", "name")

    def __init__(self, nc, name):
        self.h = nc.alloc_semaphore(name=name)
        self.count = 0
        self.name = name


class Prog:
    def __init__(self, nc):
        self.nc = nc
        self.q = {e: [] for e in ENGINES}
        self.shared = {}
        self.n_mm = 0
        self.phase_marks = []

    def mark_phase(self, name):
        self.phase_marks.append((name, self.n_mm))

    def sem(self, name):
        return Sem(self.nc, name)

    def raw(self, eng, fn):
        self.q[eng].append(fn)

    def wait(self, eng, sem, val=None):
        v = sem.count if val is None else val
        if v <= 0:
            return
        self.q[eng].append(lambda e, _h=sem.h, _v=v: e.wait_ge(_h, _v))

    def dma(self, eng, out, in_, sem, waits=(), **kw):
        for (ws, wv) in waits:
            self.wait(eng, ws, wv)
        sem.count += 16
        self.q[eng].append(
            lambda e, _o=out, _i=in_, _h=sem.h, _kw=dict(kw): e.dma_start(
                out=_o, in_=_i, **_kw
            ).then_inc(_h, 16)
        )

    def cond_dma8(self, eng, outs_ins, sem, waits=()):
        for (ws, wv) in waits:
            self.wait(eng, ws, wv)
        sem.count += 16

        def f(e, _oi=list(outs_ins), _h=sem.h):
            r = self.shared["cidreg_" + eng]
            for c, (o, i) in enumerate(_oi):
                with e.If_eq(r, c):
                    e.dma_start(out=o, in_=i).then_inc(_h, 16)

        self.q[eng].append(f)

    def op(self, eng, meth, *args, inc=None, waits=(), **kw):
        if eng == "tensor" and meth == "matmul":
            self.n_mm += 1
        if eng == "tensor" and meth == "transpose":
            self.n_tr = getattr(self, "n_tr", 0) + 1
        for (ws, wv) in waits:
            self.wait(eng, ws, wv)
        if inc is not None:
            inc[0].count += inc[1]

        def f(e, _m=meth, _a=args, _kw=dict(kw), _inc=inc):
            ins = getattr(e, _m)(*_a, **_kw)
            if _inc is not None:
                ins.then_inc(_inc[0].h, _inc[1])

        self.q[eng].append(f)

    def collective(self, kind, op, ins, outs, sem, waits=()):
        for (ws, wv) in waits:
            self.wait("gpsimd", ws, wv)
        sem.count += 1

        def f(e, _i=ins, _o=outs, _h=sem.h):
            e.collective_compute(
                kind, op, replica_groups=[list(range(NCORES))], ins=_i, outs=_o
            ).then_inc(_h, 1)

        self.q["gpsimd"].append(f)


class Bank:

    def __init__(self, p, name, ap):
        self.ap = ap
        self.done = p.sem(name + "_d")
        self.free = p.sem(name + "_f")


def build_program(upto=99, taps=()):
    nc = bass.Bass("TRN2", target_bir_lowering=False)

    def din(name, shape, dt=F32):
        return nc.dram_tensor(name, list(shape), dt, kind="ExternalInput").ap()

    x_in = din("x_in", (T, DM))
    cid_in = din("cid", (1, 1), I32)
    ident_in = din("ident", (128, 128), BF16)
    trilA_in = din("trilA", (128, 128), BF16)
    maskB_in = din("maskB", (4, 128, 512), BF16)
    ones_in = din("ones128", (128, 128), BF16)
    ones1f_in = din("ones1x128f", (1, 128), F32)
    wqa_in = din("WqA", (DM, 256))
    wka_in = din("WkA", (DM, 256))
    wva_in = din("WvA", (DM, 256))
    bqa_in = din("bqA", (1, 256))
    bka_in = din("bkA", (1, 256))
    bva_in = din("bvA", (1, 256))
    woa_in = din("WoA", (DM, DM))
    boa_in = din("boA", (1, DM))
    ln1_in = din("ln1w", (1, DM))
    ln2_in = din("ln2w", (1, DM))
    wqb_in = din("WqBT", (DM, 256))
    wkb_in = din("WkBT", (DM, 256))
    wvb_in = din("WvBT", (DM, 256))
    wob_in = din("WoB", (DM, DM))
    alora_in = din("A_lora", (DM, R))
    blora_in = din("B_lora", (R, FF))
    wint_in = din("W_inT", (DM, FF))
    wupt_in = din("W_upT", (DM, FF))
    wdnt_in = din("W_downT", (FF, DM))

    y_out = nc.dram_tensor("y", [T, DM], F32, kind="ExternalOutput").ap()

    def dint(name, shape, dt=BF16):
        return nc.dram_tensor(name, list(shape), dt, kind="Internal")

    def dshared(name, shape):
        return nc.dram_tensor(name, list(shape), BF16, kind="Internal",
                              addr_space="Shared")

    g1_in = [dint(f"g1_in{u}", (DM, 512)) for u in range(2)]
    g1_out = [dshared(f"g1_out{u}", (NCORES * DM, 512)) for u in range(2)]
    g2_in = dint("g2_in", (S, 256))
    g2_out = [dshared(f"g2_out{u}", (NCORES * S // 2, 256)) for u in range(2)]
    g3_in = [dint(f"g3_in{u}", (DM, 512)) for u in range(2)]
    g3_out = [dshared(f"g3_out{u}", (NCORES * DM, 512)) for u in range(2)]
    g4_in = dint("g4_in", (NCORES * 256, T))
    g4_out = [dshared(f"g4_out{u}", (NCORES * 1024, T)) for u in range(2)]
    x2_spill = dint("x2_spill", (T, DM), F32)
    g_dbg = dint("g_dbg", (FF, T), BF16)
    hT_spill = dint("hT_spill", (DM, T))
    mlp_acc = dint("mlp_acc", (T, DM), F32)

    tap_handles = {
        "x2_spill": x2_spill, "hT_spill": hT_spill, "mlp_acc": mlp_acc,
    }

    p = Prog(nc)

    _open = []

    _uid = [0]

    def sball(name, shape, dt):
        _uid[0] += 1
        cm = nc.sbuf_tensor(f"{name}__{_uid[0]}", list(shape), dt)
        h = cm.__enter__()
        _open.append((name, cm))
        return h

    def sbfree(*names):
        want = set(names)
        top = [n for (n, _) in _open[-len(want):]]
        assert set(top) == want, f"non-LIFO sbuf free: top={top} want={sorted(want)}"
        for _ in range(len(want)):
            n, cm = _open.pop()
            cm.__exit__(None, None, None)

    cid_sb = sball("cid_sb", (1, 1), I32)
    ident = sball("ident_sb", (128, 128), BF16)
    trilA = sball("trilA_sb", (128, 128), BF16)
    ones128 = sball("ones128_sb", (128, 128), BF16)
    ones1f = sball("ones1f", (1, 128), F32)
    xt = [sball(f"xt{i}", (128, DM), F32) for i in range(2)]
    sq_scr = sball("sq_scr", (128, DM), F32)
    n1bf = sball("n1bf", (128, DM), BF16)
    n2bf = sball("n2bf", (128, DM), BF16)
    stage = sball("stage", (128, DM), BF16)
    st8 = sball("st8", (128, 8), F32)
    SL = sball("SL", (128, 64), F32)
    stats_sb = sball("stats_sb", (128, 16), F32)
    hA_sb = sball("hA_sb", (128, TT * 16), F32)
    a_sb = sball("a_sb", (128, FC * 16), BF16)
    wcomb = sball("wcomb", (128, FC * 16), BF16)
    modhAT = sball("modhAT", (16, T), BF16)
    negB = sball("negB", (128, 2), F32)
    nrm2 = sball("nrm2", (1, 4), F32)
    att_stage = sball("att_stage", (128, 1024), BF16)
    pv_stage = sball("pv_stage", (128, 512), BF16)
    probs = [sball(f"probs{i}", (128, 1024), BF16) for i in range(2)]
    n1T_cur = [sball(f"n1T{i}", (128, FC * 512), BF16) for i in range(2)]

    ps_a = nc.psum_tensor("ps_a", [128, 1024], F32).__enter__()
    ps_b = nc.psum_tensor("ps_b", [128, 1024], F32).__enter__()
    ps_c = nc.psum_tensor("ps_c", [128, 512], F32).__enter__()
    ps_d = nc.psum_tensor("ps_d", [128, 512], F32).__enter__()
    ps_e = nc.psum_tensor("ps_e", [128, 512], F32).__enter__()
    ps_tr = nc.psum_tensor("ps_tr", [128, 1024], BF16).__enter__()

    bA = Bank(p, "bA", ps_a[:, :])
    bB = Bank(p, "bB", ps_b[:, :])
    bA0 = Bank(p, "bA0", ps_a[:, 0:512])
    bA1 = Bank(p, "bA1", ps_a[:, 512:1024])
    bB0 = Bank(p, "bB0", ps_b[:, 0:512])
    bB1 = Bank(p, "bB1", ps_b[:, 512:1024])
    bC = Bank(p, "bC", ps_c[:, :])
    bD = Bank(p, "bD", ps_d[:, :])
    bE = Bank(p, "bE", ps_e[:, :])
    bT = Bank(p, "bT", ps_tr[:, :])

    s_ld = p.sem("ld")
    s_xt = [p.sem("xt0s"), p.sem("xt1s")]
    s_n1s = [p.sem("n1s0"), p.sem("n1s1")]
    s_ata = p.sem("ata")
    s_g1 = [p.sem("g1a"), p.sem("g1b")]
    s_g2 = [p.sem("g2a"), p.sem("g2b")]
    s_g3 = [p.sem("g3a"), p.sem("g3b")]
    s_g4 = [p.sem("g4a"), p.sem("g4b")]
    s_cst = p.sem("cst")
    s_sp = p.sem("spill")
    s_gld = p.sem("gld")
    s_mm = p.sem("mm")
    s_ve = p.sem("ve")
    s_se = p.sem("se")
    s_ge = p.sem("ge")
    s_cc = p.sem("cc")
    s_acc = p.sem("acc")
    s_slab = p.sem("slab")

    def t_op(meth, *a, **k):
        p.op("tensor", meth, *a, **k)

    def v_op(meth, *a, **k):
        p.op("vector", meth, *a, **k)

    def sc_op(meth, *a, **k):
        p.op("scalar", meth, *a, **k)

    def g_op(meth, *a, **k):
        p.op("gpsimd", meth, *a, **k)

    def W(*sems):
        return [(s, None) for s in sems]

    def barrier():
        t_op("drain", inc=(s_mm, 1))
        v_op("engine_nop", inc=(s_ve, 1))
        sc_op("drain", inc=(s_se, 1))
        g_op("drain", inc=(s_ge, 1))
        allsems = [s_ld, s_gld, s_mm, s_ve, s_se, s_ge, s_acc, s_cst,
                   s_sp, s_ata, s_xt[0], s_xt[1], s_n1s[0], s_n1s[1],
                   s_g1[0], s_g1[1], s_g2[0], s_g2[1], s_g3[0], s_g3[1],
                   s_g4[0], s_g4[1]]
        for eng in ENGINES:
            for s in allsems:
                p.wait(eng, s)

    s_cid = p.sem("cid")
    p.dma("sync", cid_sb[:, :], cid_in[:, :], s_cid)
    p.dma("sync", ident[:, :], ident_in[:, :], s_cst)
    p.dma("sync", trilA[:, :], trilA_in[:, :], s_cst)
    p.dma("sync", ones128[:, :], ones_in[:, :], s_cst)
    p.dma("sync", ones1f[:, :], ones1f_in[:, :], s_cst)
    p.dma("gpsimd", a_sb[:, :].rearrange("p (fc r) -> p fc r", r=16),
          alora_in.rearrange("(fc p) r -> p fc r", p=128), s_gld)

    def _load_cid_sync(e, _ap=cid_sb[0:1, 0:1]):
        r = e.alloc_register("cidreg")
        e.reg_load(r, _ap)
        p.shared["cidreg_sync"] = r

    p.wait("sync", s_cid, 16)
    p.raw("sync", _load_cid_sync)

    def ln_stats(src_f32, eps, inv_out, m2n_out):
        mean, msq, var, tmp, ssq, s1 = (st8[:, i:i + 1] for i in range(6))
        v_op("reduce_sum", s1, src_f32, AX, inc=(s_ve, 1))
        sc_op("activation", sq_scr[:, :], src_f32, AF.Square, accum_out=ssq,
              waits=W(s_ve), inc=(s_se, 1))
        v_op("tensor_scalar", mean, s1, 1.0 / DM, None, OP.mult,
             waits=W(s_ve), inc=(s_ve, 1))
        v_op("tensor_scalar", msq, ssq, 1.0 / DM, None, OP.mult,
             waits=W(s_se), inc=(s_ve, 1))
        sc_op("activation", tmp, mean, AF.Square, waits=W(s_ve), inc=(s_se, 1))
        v_op("tensor_tensor", var, msq, tmp, OP.subtract, waits=W(s_se, s_ve),
             inc=(s_ve, 1))
        v_op("tensor_scalar", var, var, 1.0, eps, OP.mult, OP.add,
             waits=W(s_ve), inc=(s_ve, 1))
        v_op("reciprocal", var, var, waits=W(s_ve), inc=(s_ve, 1))
        sc_op("activation", inv_out, var, AF.Sqrt, waits=W(s_ve), inc=(s_se, 1))
        v_op("tensor_scalar", m2n_out, mean, inv_out, -1.0, OP.mult, OP.mult,
             waits=W(s_se), inc=(s_ve, 1))

    def transpose_tile(dst_col_ap, src_bf, extra_wait=(), sem=None,
                       guards=()):
        sem = s_ld if sem is None else sem
        for half in range(2):
            for k in range(8):
                fcc = half * 8 + k
                t_op("transpose", ps_tr[:, 128 * k:128 * (k + 1)],
                     src_bf[:, 128 * fcc:128 * (fcc + 1)], ident[:, :],
                     waits=(W(bT.free) + list(extra_wait) if k == 0 else ()),
                     inc=((bT.done, 1) if k == 7 else None))
            v_op("tensor_copy", stage[:, 1024 * half:1024 * (half + 1)],
                 ps_tr[:, :], waits=W(bT.done, sem, *guards),
                 inc=(bT.free, 1))
        v_op("engine_nop", inc=(s_ve, 1))
        p.dma("sync", dst_col_ap,
              stage[:, :].rearrange("p (fc j) -> p fc j", j=128),
              sem, waits=W(s_ve))

    g1w = [g1_in[u].ap().rearrange("(fc p) (i j) -> i p fc j",
                                   fc=16, p=128, i=4, j=128) for u in range(2)]
    cc_marks = {}

    wob_cur = [sball(f"wob_cur{i}", (128, FC * 128), BF16) for i in range(2)]
    s_wob = [p.sem("wob0"), p.sem("wob1")]
    wob_mark = {}

    def wcomb_load(kc):
        p.dma("gpsimd",
              wob_cur[kc % 2][:, :].rearrange("p (fc j) -> p fc j", j=128),
              wob_in[:, 128 * kc:128 * (kc + 1)].rearrange(
                  "(fc p) j -> p fc j", p=128),
              s_wob[kc % 2], waits=[(bE.done, wob_mark.get(kc - 2, 0))])
        p.shared[("wcl", kc)] = (s_wob[kc % 2], s_wob[kc % 2].count)

    def wcomb_unit(kc):
        if kc + 1 < FC:
            wcomb_load(kc + 1)
        for jc in range(FC):
            t_op("matmul", ps_e[:, 0:16],
                 wob_cur[kc % 2][:, :].rearrange(
                     "p (fc j) -> p fc j", j=128)[:, jc, :],
                 a_sb[:, :].rearrange("p (fc r) -> p fc r", r=16)[:, jc, :],
                 start=(jc == 0), stop=(jc == FC - 1),
                 waits=(W(bE.free) + [p.shared[("wcl", kc)], (s_gld, 16)]
                        if jc == 0 else ()),
                 inc=((bE.done, 1) if jc == FC - 1 else None))
        wob_mark[kc] = bE.done.count
        v_op("tensor_copy", wcomb[:, 16 * kc:16 * (kc + 1)], ps_e[:, 0:16],
             waits=W(bE.done), inc=(bE.free, 1))

    def batched_ln_chain(w, eps, inv_out=None, m2n_out=None):
        inv_out = SL[:, 48:48 + w] if inv_out is None else inv_out
        m2n_out = SL[:, 56:56 + w] if m2n_out is None else m2n_out
        v_op("tensor_scalar", SL[:, 16:32], SL[:, 0:16], 1.0 / DM,
             None, OP.mult, waits=W(s_ve, s_se), inc=(s_ve, 1))
        sc_op("activation", SL[:, 32:32 + w], SL[:, 16:16 + w], AF.Square,
              waits=W(s_ve), inc=(s_se, 1))
        v_op("scalar_tensor_tensor", SL[:, 40:40 + w], SL[:, 24:24 + w], eps,
             SL[:, 32:32 + w], OP.add, OP.subtract, waits=W(s_se, s_ve),
             inc=(s_ve, 1))
        v_op("reciprocal", SL[:, 40:40 + w], SL[:, 40:40 + w], waits=W(s_ve),
             inc=(s_ve, 1))
        sc_op("activation", inv_out, SL[:, 40:40 + w], AF.Sqrt,
              waits=W(s_ve), inc=(s_se, 1))
        v_op("scalar_tensor_tensor", m2n_out, SL[:, 16:16 + w], -1.0,
             inv_out, OP.mult, OP.mult, waits=W(s_se, s_ve),
             inc=(s_ve, 1))

    p.mark_phase("ph1")
    if upto >= 1:
        wcomb_load(0)
        p.wait("tensor", s_cst)
        p.wait("vector", s_cst)
        for i in range(TT):
            buf = xt[i % 2]
            p.dma("sync", buf[:, :], x_in[128 * i:128 * (i + 1), :],
                  s_xt[i % 2], waits=(W(s_ve, s_se) if i >= 2 else ()))
            p.wait("vector", s_xt[i % 2])
            p.wait("scalar", s_xt[i % 2])
            v_op("reduce_sum", SL[:, i:i + 1], buf[:, :], AX, inc=(s_ve, 1))
            sc_op("activation", sq_scr[:, :], buf[:, :], AF.Square,
                  accum_out=SL[:, 8 + i:8 + i + 1], waits=W(s_se),
                  inc=(s_se, 1))
            if upto >= 6 and i < 4:
                wcomb_unit(2 * i)
                wcomb_unit(2 * i + 1)
        batched_ln_chain(8, EPS_LN)
        for i in range(TT):
            buf = xt[i % 2]
            p.dma("sync", buf[:, :], x_in[128 * i:128 * (i + 1), :],
                  s_xt[i % 2], waits=W(s_ve))
            p.wait("vector", s_xt[i % 2])
            dstb = n1bf if i % 2 == 0 else n2bf
            v_op("tensor_scalar", dstb[:, :], buf[:, :], SL[:, 48 + i:49 + i],
                 SL[:, 56 + i:57 + i], OP.mult, OP.add,
                 waits=W(s_ve, bT.done), inc=(s_ve, 1))
            transpose_tile(g1w[i // 4][i % 4], dstb[:, :], extra_wait=W(s_ve),
                           sem=s_g1[i // 4], guards=(s_g1[1 - i // 4],))
            if upto >= 6 and i < 4:
                wcomb_unit(8 + 2 * i)
                wcomb_unit(8 + 2 * i + 1)
            if i % 4 == 3:
                p.collective("AllGather", OP.bypass,
                             [g1_in[i // 4].ap().opt()],
                             [g1_out[i // 4].ap().opt()], s_cc,
                             waits=W(s_g1[i // 4]))
                cc_marks[("g1", i // 4)] = s_cc.count
        barrier()
        sbfree("wob_cur0", "wob_cur1")

    g1v = [g1_out[u].ap().rearrange("(s fc p) tok -> s p fc tok",
                                    s=8, fc=16, p=128) for u in range(2)]
    g3v = [g3_out[u].ap().rearrange("(s fc p) tok -> s p fc tok",
                                    s=8, fc=16, p=128) for u in range(2)]

    def qk_pass(gv, wq_sb, wk_sb, elu, bqk, qT, kT, ccw):
        qk_banks = [bA0, bA1, bB0, bB1]
        marks = {}
        for tc in range(16):
            buf = n1T_cur[tc % 2]
            p.dma("sync", buf[:, :].rearrange("p (fc tok) -> p fc tok", tok=512),
                  gv[tc % 2][tc // 2], s_n1s[tc % 2],
                  waits=list(marks.get(tc - 2, ())) + [(s_cc, ccw[tc % 2])])
            p.wait("tensor", s_n1s[tc % 2])
            for wi, (wsb, dst) in enumerate(((wq_sb, qT), (wk_sb, kT))):
                for fh in range(2):
                    bk_ = qk_banks[2 * wi + fh]
                    for fcc in range(FC):
                        t_op("matmul", bk_.ap,
                             wsb[:, fcc, 128 * fh:128 * (fh + 1)],
                             buf[:, :].rearrange("p (fc tok) -> p fc tok",
                                                 tok=512)[:, fcc, :],
                             start=(fcc == 0), stop=(fcc == FC - 1),
                             waits=(W(bk_.free) if fcc == 0 else ()),
                             inc=((bk_.done, 1) if fcc == FC - 1 else None))
                    dcol = dst[:, fh * S + 512 * tc: fh * S + 512 * (tc + 1)]
                    if elu:
                        bias = bqk[:, 2 * wi + fh:2 * wi + fh + 1]
                        t1 = sq_scr[:, 0:512]
                        rr = sq_scr[:, 512:1024]
                        ee = sq_scr[:, 1024:1536]
                        v_op("tensor_scalar", t1, bk_.ap, bias, 0.0,
                             OP.add, OP.min, waits=W(bk_.done, s_se),
                             inc=(s_ve, 1))
                        v_op("tensor_scalar", rr, bk_.ap, bias, 0.0,
                             OP.add, OP.max, inc=(bk_.free, 1))
                        sc_op("activation", ee, t1, AF.Exp, waits=W(s_ve),
                              inc=(s_se, 1))
                        v_op("tensor_tensor", dcol, ee, rr, OP.add,
                             waits=W(s_se, bk_.free), inc=(s_ve, 1))
                    else:
                        v_op("tensor_copy", dcol, bk_.ap, waits=W(bk_.done),
                             inc=(bk_.free, 1))
            marks[tc] = [(b.done, b.done.count) for b in qk_banks]

    def v_pass(gv, wv_sb, bias_bc, v_sb, ccw, co_units=None):
        vb = [bC, bD]
        marks = {}
        for tc in range(16):
            buf = n1T_cur[tc % 2]
            p.dma("sync", buf[:, :].rearrange("p (fc tok) -> p fc tok", tok=512),
                  gv[tc % 2][tc // 2], s_n1s[tc % 2],
                  waits=list(marks.get(tc - 2, ())) + [(s_cc, ccw[tc % 2])])
            p.wait("tensor", s_n1s[tc % 2])
            for sub in range(4):
                ch = 4 * tc + sub
                bk_ = vb[sub % 2]
                for fcc in range(FC):
                    t_op("matmul", bk_.ap[:, 0:256],
                         buf[:, :].rearrange("p (fc tok) -> p fc tok",
                                             tok=512)[:, fcc,
                                                      128 * sub:128 * (sub + 1)],
                         wv_sb[:, fcc, :],
                         start=(fcc == 0), stop=(fcc == FC - 1),
                         waits=(W(bk_.free) if fcc == 0 else ()),
                         inc=((bk_.done, 1) if fcc == FC - 1 else None))
                dst = v_sb[:, ch, :, 0:128]
                src = bk_.ap[:, 0:256].rearrange("p (h e) -> p h e", h=2, e=128)
                if bias_bc is not None:
                    v_op("tensor_tensor", dst, src,
                         bias_bc[:, :].rearrange("p (h e) -> p h e", h=2, e=128),
                         OP.add, waits=W(bk_.done), inc=(bk_.free, 1))
                else:
                    v_op("tensor_copy", dst, src, waits=W(bk_.done),
                         inc=(bk_.free, 1))
                if co_units is not None:
                    co_units[4 * tc + sub]()
            marks[tc] = [(b.done, b.done.count) for b in vb]

    p.mark_phase("ph2")
    if upto >= 2:
        qT = sball("qT", (128, 2 * S), BF16)
        kT = sball("kT", (128, 2 * S), BF16)
        v4 = sball("v4", (128, CH * 2 * 132), BF16)
        v_sb = v4[:, :].rearrange("p (ch h e) -> p ch h e", ch=CH, h=2)
        wq_sb = sball("wq_sb", (128, FC * 256), BF16)
        wk_sb = sball("wk_sb", (128, FC * 256), BF16)
        bqk_sb = sball("bqk_sb", (128, 4), F32)

        p.dma("gpsimd", wq_sb[:, :].rearrange("p (fc j) -> p fc j", j=256),
              wqa_in.rearrange("(fc p) j -> p fc j", p=128), s_gld)
        p.dma("gpsimd", wk_sb[:, :].rearrange("p (fc j) -> p fc j", j=256),
              wka_in.rearrange("(fc p) j -> p fc j", p=128), s_gld)
        p.dma("sync", bqk_sb[:, 0:2], bqa_in.rearrange("o (c p) -> (o p) c", p=128),
              s_cst, allow_slow_non_contiguous=True)
        p.dma("sync", bqk_sb[:, 2:4], bka_in.rearrange("o (c p) -> (o p) c", p=128),
              s_cst, allow_slow_non_contiguous=True)
        g_op("memset", v_sb[:, :, :, 128:129], 1.0, inc=(s_ge, 1))
        p.wait("tensor", s_gld)
        p.wait("vector", s_cst)
        qk_pass(g1v, wq_sb[:, :].rearrange("p (fc j) -> p fc j", j=256),
                wk_sb[:, :].rearrange("p (fc j) -> p fc j", j=256),
                True, bqk_sb, qT, kT,
                {0: cc_marks[("g1", 0)], 1: cc_marks[("g1", 1)]})
        barrier()
        wv_sb = wq_sb
        p.dma("gpsimd", wv_sb[:, :].rearrange("p (fc j) -> p fc j", j=256),
              wva_in.rearrange("(fc p) j -> p fc j", p=128), s_gld)
        bv_bc = sball("bv_bc", (128, 256), F32)
        p.dma("sync", bv_bc[:, :], bva_in.broadcast_to((128, 256)), s_cst)
        p.wait("tensor", s_gld)
        p.wait("vector", s_cst)
        v_pass(g1v, wv_sb[:, :].rearrange("p (fc j) -> p fc j", j=256),
               bv_bc, v_sb, {0: cc_marks[("g1", 0)], 1: cc_marks[("g1", 1)]})
        barrier()

    p.mark_phase("ph3")
    if upto >= 3:
        S_ext = sball("S_ext", (128, 2 * 132), F32)
        S_bf = sball("S_bf", (128, 2 * 132), BF16)
        kf_tok = sball("kf_tok", (128, 2 * 128), BF16)
        AT_bf = sball("AT_bf", (128, 2 * 128), BF16)
        g_op("memset", S_ext[:, :], 0.0, inc=(s_ge, 1))
        g_op("memset", S_bf[:, :], 0.0, inc=(s_ge, 1))
        p.wait("tensor", s_ge)
        p.wait("vector", s_ge)
        g2w = g2_in.ap().rearrange("(cb cc p) (h e) -> cb p cc h e",
                                   cb=16, cc=4, p=128, h=2, e=128)
        hbanks = [(bC, bD, bE), (bA0, bA1, bB0)]
        for ch in range(CH):
            for h in range(HPC):
                bAT, bKV, bOUT = hbanks[h]
                Se = S_ext[:, 132 * h:132 * h + 129]
                Sb = S_bf[:, 132 * h:132 * h + 129]
                kft = kf_tok[:, 128 * h:128 * (h + 1)]
                atb = AT_bf[:, 128 * h:128 * (h + 1)]
                qcol = qT[:, h * S + 128 * ch: h * S + 128 * (ch + 1)]
                kcol = kT[:, h * S + 128 * ch: h * S + 128 * (ch + 1)]
                vext = v_sb[:, ch, h, 0:129]
                t_op("transpose", ps_tr[:, 128 * h:128 * (h + 1)], kcol,
                     ident[:, :], waits=W(bT.free), inc=(bT.done, 1))
                v_op("tensor_copy", kft, ps_tr[:, 128 * h:128 * (h + 1)],
                     waits=W(bT.done), inc=(bT.free, 1))
                t_op("matmul", bAT.ap[:, 0:128], kcol, qcol, start=True,
                     stop=True, waits=W(bAT.free, s_ve), inc=(bAT.done, 1))
                v_op("tensor_tensor", atb, bAT.ap[:, 0:128], trilA[:, :],
                     OP.mult, waits=W(bAT.done), inc=(bAT.free, 1))
                v_op("engine_nop", inc=(s_ve, 1))
                t_op("matmul", bOUT.ap[:, 0:129], qcol, Sb, start=True,
                     stop=False, waits=W(bOUT.free, s_ve))
                t_op("matmul", bOUT.ap[:, 0:129], atb, vext, start=False,
                     stop=True, inc=(bOUT.done, 1))
                t_op("matmul", bKV.ap[:, 0:129], kft, vext, start=True,
                     stop=True, waits=W(bKV.free), inc=(bKV.done, 1))
                v_op("tensor_tensor", Se, Se, bKV.ap[:, 0:129], OP.add,
                     waits=W(bKV.done), inc=(bKV.free, 1))
                v_op("tensor_copy", Sb, Se, inc=(s_ve, 1))
                zt = st8[:, h:h + 1]
                v_op("tensor_scalar", zt, bOUT.ap[:, 128:129], 1.0, EPS_A,
                     OP.mult, OP.add, waits=W(bOUT.done), inc=(s_ve, 1))
                v_op("reciprocal", zt, zt, waits=W(s_ve), inc=(s_ve, 1))
                dst = att_stage[:, :].rearrange("p (cc h e) -> p cc h e",
                                                cc=4, h=2, e=128)[:, ch % 4, h, :]
                v_op("tensor_scalar", dst, bOUT.ap[:, 0:128], zt, None, OP.mult,
                     waits=W(s_ld, s_ve), inc=(bOUT.free, 1))
                v_op("engine_nop", inc=(s_ve, 1))
            if ch % 4 == 3:
                p.dma("sync", g2w[ch // 4],
                      att_stage[:, :].rearrange("p (cc h e) -> p cc h e",
                                                cc=4, h=2, e=128),
                      s_g2[ch // 32], waits=W(s_ve))
            if ch == 31:
                p.collective("AllGather", OP.bypass,
                             [g2_in.ap()[0:4096, :].opt()],
                             [g2_out[0].ap().opt()], s_cc, waits=W(s_g2[0]))
                cc_marks[("g2", 0)] = s_cc.count
        p.collective("AllGather", OP.bypass, [g2_in.ap()[4096:8192, :].opt()],
                     [g2_out[1].ap().opt()], s_cc, waits=W(s_g2[1]))
        cc_marks[("g2", 1)] = s_cc.count
        barrier()
        sbfree("S_ext", "S_bf", "kf_tok", "AT_bf")

    p.mark_phase("ph45")
    if upto >= 4:
        sbfree("qT", "kT", "v4", "wq_sb", "wk_sb", "bqk_sb", "bv_bc")
        attnAT = [sball(f"attnAT{i}", (128, DM), BF16) for i in range(4)]
        attnA_t = sball("attnA_t", (128, DM), BF16)
        wo_t = [sball(f"wo_t{i}", (128, DM), F32) for i in range(4)]
        woa_cur = [sball(f"woa_cur{i}", (128, FC * 512), BF16)
                   for i in range(2)]
        bo_bc = sball("bo_bc", (128, DM), F32)
        ln12_bc = sball("ln12_bc", (128, 2 * DM), F32)
        t_scr = sball("t_scr", (128, DM), F32)

        p.dma("sync", bo_bc[:, :], boa_in.broadcast_to((128, DM)), s_cst)
        p.dma("sync", ln12_bc[:, 0:DM], ln1_in.broadcast_to((128, DM)), s_cst)
        p.dma("sync", ln12_bc[:, DM:2 * DM], ln2_in.broadcast_to((128, DM)),
              s_cst)
        p.wait("vector", s_cst)

        g2v = [g2_out[u].ap().rearrange("(s c i p) (h e) -> c i p s h e",
                                        s=8, c=4, i=8, p=128, h=2, e=128)
               for u in range(2)]
        g3w = [g3_in[u].ap().rearrange("(fc p) (i j) -> i p fc j",
                                       fc=16, p=128, i=4, j=128)
               for u in range(2)]
        hTw = hT_spill.ap().rearrange("(fc p) (i j) -> i p fc j", fc=16, p=128, i=8, j=128)
        woav = [woa_cur[i][:, :].rearrange("p (fc j) -> p fc j", j=512)
                for i in range(2)]
        wo_banks = [bA0, bA1, bB0, bB1]
        woa_marks = {}

        s_woa = [p.sem("woa0"), p.sem("woa1")]

        def load_woa(th, fo):
            k = 4 * th + fo
            p.dma("gpsimd", woav[k % 2],
                  woa_in[:, 512 * fo:512 * (fo + 1)].rearrange(
                      "(fc p) j -> p fc j", p=128),
                  s_woa[k % 2], waits=woa_marks.get(k - 2, ()))
            p.shared[("woal", k)] = (s_woa[k % 2], s_woa[k % 2].count)

        att_free = {}
        for th in range(2):
            for i in range(4):
                gi = 4 * th + i
                p.cond_dma8(
                    "sync",
                    [(attnA_t[:, :].rearrange("p (s h e) -> p s h e", s=8, h=2, e=128),
                      g2v[c // 4][c % 4, gi]) for c in range(8)],
                    s_ata,
                    waits=W(bT.done) + [(s_cc, cc_marks[("g2", 1)])])
                for half in range(2):
                    for k in range(8):
                        fcc = half * 8 + k
                        t_op("transpose", ps_tr[:, 128 * k:128 * (k + 1)],
                             attnA_t[:, 128 * fcc:128 * (fcc + 1)], ident[:, :],
                             waits=(W(bT.free, s_ata) if k == 0 else ()),
                             inc=((bT.done, 1) if k == 7 else None))
                    v_op("tensor_copy", attnAT[i][:, 1024 * half:1024 * (half + 1)],
                         ps_tr[:, :],
                         waits=W(bT.done) + att_free.get(i, []),
                         inc=(bT.free, 1))
                v_op("engine_nop", inc=(s_ve, 1))
            if th == 0:
                load_woa(0, 0)
            for fo in range(4):
                k = 4 * th + fo
                if fo + 1 < 4:
                    load_woa(th, fo + 1)
                elif th == 0:
                    load_woa(1, 0)
                p.raw("tensor", (lambda e, _w=p.shared[("woal", k)]:
                                 e.wait_ge(_w[0].h, _w[1])))
                p.wait("tensor", s_ve)
                for i in range(4):
                    bk_ = wo_banks[i]
                    for fcc in range(FC):
                        t_op("matmul", bk_.ap,
                             attnAT[i][:, 128 * fcc:128 * (fcc + 1)],
                             woav[k % 2][:, fcc, :],
                             start=(fcc == 0), stop=(fcc == FC - 1),
                             waits=(W(bk_.free) if fcc == 0 else ()),
                             inc=((bk_.done, 1) if fcc == FC - 1 else None))
                    v_op("tensor_tensor", wo_t[i][:, 512 * fo:512 * (fo + 1)],
                         bk_.ap, bo_bc[:, 512 * fo:512 * (fo + 1)], OP.add,
                         waits=W(bk_.done), inc=(bk_.free, 1))
                woa_marks[k] = [(b.done, b.done.count) for b in wo_banks]
                if fo == 3:
                    for i2 in range(4):
                        att_free[i2] = [(wo_banks[i2].done,
                                         wo_banks[i2].done.count)]
            for i in range(4):
                v_op("reduce_sum", SL[:, i:i + 1], wo_t[i][:, :], AX,
                     inc=(s_ve, 1))
                sc_op("activation", sq_scr[:, :], wo_t[i][:, :], AF.Square,
                      accum_out=SL[:, 8 + i:8 + i + 1], waits=W(s_ve, s_se),
                      inc=(s_se, 1))
            batched_ln_chain(4, EPS_LN)
            for i in range(4):
                gi = 4 * th + i
                buf = xt[i % 2]
                p.dma("sync", buf[:, :], x_in[128 * gi:128 * (gi + 1), :],
                      s_xt[i % 2], waits=W(s_ve))
                p.wait("vector", s_xt[i % 2])
                v_op("tensor_scalar", wo_t[i][:, :], wo_t[i][:, :],
                     SL[:, 48 + i:49 + i], SL[:, 56 + i:57 + i],
                     OP.mult, OP.add, waits=W(s_se, s_ve), inc=(s_ve, 1))
                v_op("tensor_tensor", wo_t[i][:, :], wo_t[i][:, :], buf[:, :],
                     OP.add, inc=(s_ve, 1))
                v_op("engine_nop", inc=(s_ve, 1))
                p.dma("sync", x2_spill.ap()[128 * gi:128 * (gi + 1), :],
                      wo_t[i][:, :], s_sp, waits=W(s_ve))
                v_op("reduce_sum", SL[:, i:i + 1], wo_t[i][:, :], AX,
                     inc=(s_ve, 1))
                sc_op("activation", sq_scr[:, :], wo_t[i][:, :], AF.Square,
                      accum_out=SL[:, 8 + i:8 + i + 1], waits=W(s_ve, s_se),
                      inc=(s_se, 1))
            batched_ln_chain(4, EPS_LN,
                             inv_out=stats_sb[:, 4 * th:4 * th + 4],
                             m2n_out=stats_sb[:, 8 + 4 * th:12 + 4 * th])
            sc_op("activation", SL[:, 32:36], stats_sb[:, 4 * th:4 * th + 4],
                  AF.Square, waits=W(s_ve), inc=(s_se, 1))
            v_op("tensor_scalar", SL[:, 32:36], SL[:, 32:36], -EPS_LN,
                 1.0 + EPS_RMS, OP.mult, OP.add, waits=W(s_se), inc=(s_ve, 1))
            v_op("reciprocal", SL[:, 32:36], SL[:, 32:36], waits=W(s_ve),
                 inc=(s_ve, 1))
            sc_op("activation", SL[:, 36:40], SL[:, 32:36], AF.Sqrt,
                  waits=W(s_ve), inc=(s_se, 1))
            for i in range(4):
                gi = 4 * th + i
                rmsv = SL[:, 36 + i:37 + i]
                v_op("tensor_scalar", wo_t[i][:, :], wo_t[i][:, :],
                     stats_sb[:, gi:gi + 1], stats_sb[:, 8 + gi:8 + gi + 1],
                     OP.mult, OP.add, waits=W(s_se, s_ve), inc=(s_ve, 1))
                v_op("scalar_tensor_tensor", n1bf[:, :], wo_t[i][:, :], rmsv,
                     ln12_bc[:, 0:DM], OP.mult, OP.mult,
                     waits=W(bT.done), inc=(s_ve, 1))
                v_op("scalar_tensor_tensor", n2bf[:, :], wo_t[i][:, :], rmsv,
                     ln12_bc[:, DM:2 * DM], OP.mult, OP.mult, inc=(s_ve, 1))
                transpose_tile(g3w[gi // 4][gi % 4], n1bf[:, :],
                               extra_wait=W(s_ve), sem=s_g3[gi // 4],
                               guards=(s_g3[1 - gi // 4], s_sp))
                for half in range(2):
                    for k in range(8):
                        fcc = half * 8 + k
                        t_op("transpose", ps_tr[:, 128 * k:128 * (k + 1)],
                             n2bf[:, 128 * fcc:128 * (fcc + 1)], ident[:, :],
                             waits=(W(bT.free, s_ve) if k == 0 else ()),
                             inc=((bT.done, 1) if k == 7 else None))
                    v_op("tensor_copy", stage[:, 1024 * half:1024 * (half + 1)],
                         ps_tr[:, :],
                         waits=W(bT.done, s_sp, s_g3[0], s_g3[1]),
                         inc=(bT.free, 1))
                v_op("engine_nop", inc=(s_ve, 1))
                p.dma("sync", hTw[gi],
                      stage[:, :].rearrange("p (fc j) -> p fc j", j=128),
                      s_sp, waits=W(s_ve))
                for fcc in range(FC):
                    t_op("matmul", ps_e[:, 0:16],
                         stage[:, 128 * fcc:128 * (fcc + 1)],
                         a_sb[:, :].rearrange("p (fc r) -> p fc r", r=16)[:, fcc, :],
                         start=(fcc == 0), stop=(fcc == FC - 1),
                         waits=(W(bE.free, s_ve) if fcc == 0 else ()),
                         inc=((bE.done, 1) if fcc == FC - 1 else None))
                v_op("tensor_copy", hA_sb[:, 16 * gi:16 * (gi + 1)],
                     ps_e[:, 0:16], waits=W(bE.done), inc=(bE.free, 1))
            if th == 0:
                p.collective("AllGather", OP.bypass, [g3_in[0].ap().opt()],
                             [g3_out[0].ap().opt()], s_cc, waits=W(s_g3[0]))
                cc_marks[("g3", 0)] = s_cc.count
        p.collective("AllGather", OP.bypass, [g3_in[1].ap().opt()],
                     [g3_out[1].ap().opt()], s_cc, waits=W(s_g3[1]))
        cc_marks[("g3", 1)] = s_cc.count
        barrier()
        sbfree("attnAT0", "attnAT1", "attnAT2", "attnAT3", "attnA_t",
               "wo_t0", "wo_t1", "wo_t2", "wo_t3", "woa_cur0", "woa_cur1", "bo_bc",
               "ln12_bc", "t_scr")

    p.mark_phase("ph6")
    if upto >= 6:
        wob_cur = [sball(f"wob_cur{i}", (128, FC * 128), BF16)
                   for i in range(2)]
        wob_mark = {}
        for kc in range(FC):
            p.dma("gpsimd",
                  wob_cur[kc % 2][:, :].rearrange("p (fc j) -> p fc j", j=128),
                  wob_in[:, 128 * kc:128 * (kc + 1)].rearrange(
                      "(fc p) j -> p fc j", p=128),
                  s_gld, waits=[(bE.done, wob_mark.get(kc - 2, 0))])
            gmark = s_gld.count
            for jc in range(FC):
                t_op("matmul", ps_e[:, 0:16],
                     wob_cur[kc % 2][:, :].rearrange(
                         "p (fc j) -> p fc j", j=128)[:, jc, :],
                     a_sb[:, :].rearrange("p (fc r) -> p fc r", r=16)[:, jc, :],
                     start=(jc == 0), stop=(jc == FC - 1),
                     waits=(W(bE.free) + [(s_gld, gmark)] if jc == 0 else ()),
                     inc=((bE.done, 1) if jc == FC - 1 else None))
            wob_mark[kc] = bE.done.count
            v_op("tensor_copy", wcomb[:, 16 * kc:16 * (kc + 1)], ps_e[:, 0:16],
                 waits=W(bE.done), inc=(bE.free, 1))
        barrier()
        sbfree("wob_cur0", "wob_cur1")
        qT = sball("qT", (128, 2 * S), BF16)
        kT = sball("kT", (128, 2 * S), BF16)
        v4 = sball("v4", (128, CH * 2 * 132), BF16)
        v_sb = v4[:, :].rearrange("p (ch h e) -> p ch h e", ch=CH, h=2)
        g_op("memset", v_sb[:, :, :, 128:129], 1.0, inc=(s_ge, 1))
        p.wait("vector", s_ge)
        wq_sb = sball("wq_sb", (128, FC * 256), BF16)
        wk_sb = sball("wk_sb", (128, FC * 256), BF16)
        p.dma("gpsimd", wq_sb[:, :].rearrange("p (fc j) -> p fc j", j=256),
              wqb_in.rearrange("(fc p) j -> p fc j", p=128), s_gld)
        p.dma("gpsimd", wk_sb[:, :].rearrange("p (fc j) -> p fc j", j=256),
              wkb_in.rearrange("(fc p) j -> p fc j", p=128), s_gld)
        p.wait("tensor", s_gld)
        qk_pass(g3v, wq_sb[:, :].rearrange("p (fc j) -> p fc j", j=256),
                wk_sb[:, :].rearrange("p (fc j) -> p fc j", j=256),
                False, None, qT, kT,
                {0: cc_marks[("g3", 0)], 1: cc_marks[("g3", 1)]})
        barrier()
        wv_sb = wq_sb
        p.dma("gpsimd", wv_sb[:, :].rearrange("p (fc j) -> p fc j", j=256),
              wvb_in.rearrange("(fc p) j -> p fc j", p=128), s_gld)
        p.wait("tensor", s_gld)
        v_pass(g3v, wv_sb[:, :].rearrange("p (fc j) -> p fc j", j=256),
               None, v_sb,
               {0: cc_marks[("g3", 0)], 1: cc_marks[("g3", 1)]})
        barrier()
        knscr = sball("knscr", (128, 512), BF16)
        g_op("memset", nrm2[:, :], 0.0, inc=(s_ge, 1))
        p.wait("vector", s_ge)
        for (ti, tens) in ((0, qT), (1, kT)):
            for h in range(HPC):
                for tc in range(16):
                    col = tens[:, h * S + 512 * tc: h * S + 512 * (tc + 1)]
                    sc_op("activation", knscr[:, :], col, AF.Square,
                          waits=W(s_ve, bE.done), inc=(s_se, 1))
                    t_op("matmul", ps_e[0:1, 0:512], ones128[:, 0:1], knscr[:, :],
                         start=True, stop=True, waits=W(s_se, bE.free),
                         inc=(bE.done, 1))
                    v_op("reduce_max", st8[0:1, 0:1], ps_e[0:1, 0:512], AX,
                         waits=W(bE.done), inc=(bE.free, 1))
                    v_op("engine_nop", inc=(s_ve, 1))
                    v_op("tensor_tensor", nrm2[0:1, 2 * ti + h:2 * ti + h + 1],
                         nrm2[0:1, 2 * ti + h:2 * ti + h + 1], st8[0:1, 0:1],
                         OP.max, waits=W(s_ve), inc=(s_ve, 1))
        v_op("tensor_tensor", nrm2[0:1, 0:2], nrm2[0:1, 0:2], nrm2[0:1, 2:4],
             OP.mult, waits=W(s_ve), inc=(s_ve, 1))
        sc_op("activation", nrm2[0:1, 0:2], nrm2[0:1, 0:2], AF.Sqrt,
              waits=W(s_ve), inc=(s_se, 1))
        v_op("tensor_scalar", nrm2[0:1, 0:2], nrm2[0:1, 0:2], -SCALE_B, None,
             OP.mult, waits=W(s_se), inc=(s_ve, 1))
        p.wait("tensor", s_ve)
        t_op("matmul", ps_e[:, 0:2], ones1f[:, :], nrm2[0:1, 0:2],
             start=True, stop=True, waits=W(bE.free), inc=(bE.done, 1))
        v_op("tensor_copy", negB[:, 0:2], ps_e[:, 0:2], waits=W(bE.done),
             inc=(bE.free, 1))
        barrier()
        sbfree("knscr")

    p.mark_phase("ph7")
    if upto >= 7:
        maskB_sb = sball("maskB_sb", (128, 4 * 512), BF16)
        rb_sb = sball("rb_sb", (128, 512), F32)
        p.dma("sync", maskB_sb[:, :].rearrange("p (m n) -> p m n", m=4),
              maskB_in.rearrange("m p n -> p m n"), s_cst)
        recip_sb = sball("recip_sb", (1, 512), F32)
        g4wv = g4_in.ap().rearrange("(h sh e) (u q) -> sh h u e q",
                                    h=2, sh=8, e=128, u=2, q=512)
        p.wait("vector", s_cst)
        p.wait("scalar", s_ge)
        sc_banks = [bA, bB]
        for h in range(HPC):
            for g in range(NG):
                qcols = qT[:, h * S + 512 * g: h * S + 512 * (g + 1)]
                npair = 2 * g + 2
                ready_se = {}
                ready_ve = {}
                for kbp in range(npair + 1):
                    if kbp < npair:
                        bk_ = sc_banks[kbp % 2]
                        for half in range(2):
                            kb = 2 * kbp + half
                            t_op("matmul", bk_.ap[:, 512 * half:512 * (half + 1)],
                                 kT[:, h * S + 128 * kb: h * S + 128 * (kb + 1)],
                                 qcols, start=True, stop=True,
                                 waits=([(s_se, p.shared.get(
                                     ("expmark", kbp % 2), 0))]
                                        if half == 0 else ()),
                                 inc=((bk_.done, 1) if half == 1 else None))
                        slab = probs[kbp % 2]
                        sc_op("activation", slab[:, :], bk_.ap, AF.Exp,
                              bias=negB[:, h:h + 1], scale=SCALE_B,
                              waits=W(bk_.done, s_slab), inc=(s_se, 1))
                        p.shared[("expmark", kbp % 2)] = s_se.count
                        ready_se[kbp] = s_se.count
                        kb0 = 2 * kbp
                        for half in range(2):
                            kb = kb0 + half
                            if 4 * g <= kb <= 4 * g + 3:
                                j = kb - 4 * g
                                v_op("tensor_tensor",
                                     slab[:, 512 * half:512 * (half + 1)],
                                     slab[:, 512 * half:512 * (half + 1)],
                                     maskB_sb[:, 512 * j:512 * (j + 1)],
                                     OP.mult,
                                     waits=[(s_se, ready_se[kbp])],
                                     inc=(s_ve, 1))
                                ready_ve[kbp] = s_ve.count
                    if kbp >= 1:
                        pk = kbp - 1
                        slab = probs[pk % 2]
                        if pk in ready_ve:
                            wl = [(s_ve, ready_ve[pk])]
                        else:
                            wl = [(s_se, ready_se[pk])]
                        for half in range(2):
                            kb = 2 * pk + half
                            t_op("matmul", ps_c[:, 0:512],
                                 v_sb[:, kb, h, 0:128],
                                 slab[:, 512 * half:512 * (half + 1)],
                                 start=(kb == 0), stop=(kb == npair * 2 - 1),
                                 skip_group_check=True,
                                 waits=(wl if half == 0 else ()))
                            t_op("matmul", ps_d[:, 0:512],
                                 ones128[:, :],
                                 slab[:, 512 * half:512 * (half + 1)],
                                 start=(kb == 0), stop=(kb == npair * 2 - 1),
                                 skip_group_check=True,
                                 inc=((s_slab, 1) if half == 1 else None))
                v_op("tensor_scalar", recip_sb[:, :], ps_d[0:1, 0:512], 1.0,
                     1e-30, OP.mult, OP.add, waits=W(s_slab), inc=(s_ve, 1))
                v_op("reciprocal", recip_sb[:, :], recip_sb[:, :],
                     waits=W(s_ve), inc=(s_ve, 1))
                p.wait("tensor", s_ve)
                t_op("matmul", ps_e[:, 0:512], ones1f[:, :], recip_sb[:, :],
                     start=True, stop=True, waits=W(bE.free), inc=(bE.done, 1))
                v_op("tensor_copy", rb_sb[:, :], ps_e[:, 0:512],
                     waits=W(bE.done), inc=(bE.free, 1))
                v_op("tensor_tensor", pv_stage[:, :], ps_c[:, 0:512],
                     rb_sb[:, :], OP.mult, waits=W(s_g4[0], s_g4[1]),
                     inc=(s_ve, 1))
                p.dma("sync", g4wv[g // 2, h, g % 2], pv_stage[:, :],
                      s_g4[h], waits=W(s_ve))
                p.wait("tensor", s_ve)
            if h == 0:
                p.collective("AllGather", OP.bypass,
                             [g4_in.ap()[0:1024, :].opt()],
                             [g4_out[0].ap().opt()], s_cc, waits=W(s_g4[0]))
                cc_marks[("g4", 0)] = s_cc.count
        p.collective("AllGather", OP.bypass, [g4_in.ap()[1024:2048, :].opt()],
                     [g4_out[1].ap().opt()], s_cc, waits=W(s_g4[1]))
        cc_marks[("g4", 1)] = s_cc.count
        barrier()
        sbfree("maskB_sb", "rb_sb", "recip_sb")

    p.mark_phase("ph8")
    if upto >= 8:
        sbfree("wq_sb", "wk_sb", "qT", "kT", "v4")
        hT_sb = sball("hT_sb", (128, FC * T), BF16)
        b_sb = sball("b_sb", (16, DM), BF16)
        mod_scr = sball("mod_scr", (128, 16), F32)
        attnBT = [sball(f"attnBT{i}", (128, DM), BF16) for i in range(2)]
        g4v = [g4_out[u].ap().rearrange("(s c e) (ti tok) -> c ti e s tok",
                                        s=8, c=8, e=128, ti=8, tok=128)
               for u in range(2)]

        s_bg = p.sem("bgld")
        s_hTl = p.sem("hTl")
        p.dma("sync", hT_sb[:, :].rearrange("p (fc t) -> p fc t", t=T),
              hT_spill.ap().rearrange("(fc p) t -> p fc t", fc=16, p=128),
              s_hTl)
        mod_done = {}

        s_abt = [p.sem("abt0"), p.sem("abt1")]

        def read_attnBT(gi):
            for u in range(2):
                p.cond_dma8(
                    "sync",
                    [(attnBT[gi % 2][:, :].rearrange(
                        "p (s h tok) -> p s h tok",
                        s=8, h=2, tok=128)[:, :, u, :],
                      g4v[u][c, gi]) for c in range(8)],
                    s_abt[gi % 2],
                    waits=[(bE.done, mod_done.get(gi - 2, 0)),
                           (s_cc, cc_marks[("g4", u)])])

        read_attnBT(0)
        for gi in range(TT):
            if gi + 1 < TT:
                read_attnBT(gi + 1)
            p.wait("tensor", s_abt[gi % 2])
            for fcc in range(FC):
                t_op("matmul", ps_e[:, 0:16],
                     attnBT[gi % 2][:, 128 * fcc:128 * (fcc + 1)],
                     wcomb[:, 16 * fcc:16 * (fcc + 1)],
                     start=(fcc == 0), stop=(fcc == FC - 1),
                     waits=(W(bE.free) if fcc == 0 else ()),
                     inc=((bE.done, 1) if fcc == FC - 1 else None))
            mod_done[gi] = bE.done.count
            sc_op("activation", mod_scr[:, :], ps_e[:, 0:16], AF.Tanh,
                  waits=W(bE.done), inc=(s_se, 1))
            v_op("tensor_scalar", mod_scr[:, :], mod_scr[:, :], MOD_SCALE, None,
                 OP.mult, waits=W(s_se), inc=(s_ve, 1))
            v_op("engine_nop", inc=(bE.free, 1))
            v_op("tensor_tensor", mod_scr[:, :], mod_scr[:, :],
                 hA_sb[:, 16 * gi:16 * (gi + 1)], OP.mult, waits=W(s_ve),
                 inc=(s_ve, 1))
            v_op("tensor_copy", n1bf[:, 0:16], mod_scr[:, :], waits=W(s_ve),
                 inc=(s_ve, 1))
            p.wait("tensor", s_ve)
            t_op("transpose", ps_tr[0:16, 0:128], n1bf[:, 0:16], ident[:, :],
                 waits=W(bT.free), inc=(bT.done, 1))
            v_op("tensor_copy", modhAT[0:16, 128 * gi:128 * (gi + 1)],
                 ps_tr[0:16, 0:128], waits=W(bT.done), inc=(bT.free, 1))
        barrier()
        sbfree("attnBT0", "attnBT1")

        gT_grp = sball("gT_grp", (128, 16 * T), BF16)
        win_cur = [sball(f"win{i}", (128, FC * 128), BF16) for i in range(2)]
        wup_cur = [sball(f"wup{i}", (128, FC * 128), BF16) for i in range(2)]
        wdn_slab = [sball(f"wdn{i}", (128, 16 * 512), BF16) for i in range(2)]
        sg_bf = sball("sg_bf", (128, 1024), BF16)
        dstage = sball("dstage", (128, 4 * 512), F32)
        gv_grp = gT_grp[:, :].rearrange("p (fc t) -> p fc t", t=T)
        hTv = hT_sb[:, :].rearrange("p (fc t) -> p fc t", t=T)

        silu_mark = {}
        mult_mark = {}
        wmarks = {}
        s_wio = [p.sem("wio0"), p.sem("wio1")]
        s_dcons = p.sem("dcons")
        down_mark = [0]
        slab_rd = {}
        slab_ld = {}

        s_wdn = [p.sem("wdn0"), p.sem("wdn1")]

        def load_wdn_slab(idx):
            grp_, fo_ = divmod(idx, 4)
            p.dma("gpsimd",
                  wdn_slab[idx % 2][:, :].rearrange("p (fc j) -> p fc j", j=512),
                  wdnt_in[2048 * grp_:2048 * (grp_ + 1),
                          512 * fo_:512 * (fo_ + 1)].rearrange(
                      "(fc p) j -> p fc j", p=128),
                  s_wdn[idx % 2], waits=[(s_dcons, slab_rd.get(idx - 2, 0))])
            slab_ld[idx] = (s_wdn[idx % 2], s_wdn[idx % 2].count)
        for grp in range(4):
            load_wdn_slab(4 * grp)
            p.dma("gpsimd", b_sb[:, :],
                  blora_in[:, 2048 * grp:2048 * (grp + 1)], s_bg,
                  waits=[(s_se, silu_mark.get(id(bA1), 0))])
            for fl in range(16):
                gfl = 16 * grp + fl
                Fcg = 16 * grp + fl
                wi = win_cur[fl % 2]
                wu = wup_cur[fl % 2]
                sw = s_wio[fl % 2]
                p.dma("gpsimd", wi[:, :].rearrange("p (dc j) -> p dc j", j=128),
                      wint_in[:, 128 * Fcg:128 * (Fcg + 1)].rearrange(
                          "(dc p) j -> p dc j", p=128),
                      sw, waits=wmarks.get(gfl - 2, ()))
                p.dma("gpsimd", wu[:, :].rearrange("p (dc j) -> p dc j", j=128),
                      wupt_in[:, 128 * Fcg:128 * (Fcg + 1)].rearrange(
                          "(dc p) j -> p dc j", p=128), sw)
                p.wait("tensor", sw)
                p.wait("tensor", s_bg)
                p.wait("tensor", s_hTl)
                for u in range(2):
                    bg = [bA0, bA1][u]
                    bu = [bB0, bB1][u]
                    for dc in range(FC):
                        t_op("matmul", bg.ap,
                             wi[:, :].rearrange("p (dc j) -> p dc j",
                                                j=128)[:, dc, :],
                             hTv[:, dc, 512 * u:512 * (u + 1)],
                             start=(dc == 0), stop=False,
                             waits=([(s_se, silu_mark.get(id(bg), 0))]
                                    + W(bg.free) if dc == 0 else ()))
                    t_op("matmul", bg.ap, b_sb[:, 128 * fl:128 * (fl + 1)],
                         modhAT[:, 512 * u:512 * (u + 1)],
                         start=False, stop=True, inc=(bg.done, 1))
                    for dc in range(FC):
                        t_op("matmul", bu.ap,
                             wu[:, :].rearrange("p (dc j) -> p dc j",
                                                j=128)[:, dc, :],
                             hTv[:, dc, 512 * u:512 * (u + 1)],
                             start=(dc == 0), stop=(dc == FC - 1),
                             waits=([(s_ve, mult_mark.get(id(bu), 0))]
                                    if dc == 0 else ()),
                             inc=((bu.done, 1) if dc == FC - 1 else None))
                    sc_op("activation", sg_bf[:, 512 * u:512 * (u + 1)], bg.ap,
                          AF.Silu,
                          waits=W(bg.done) + [(s_ve,
                                               mult_mark.get(id(bu), 0))],
                          inc=(s_se, 1))
                    silu_mark[id(bg)] = s_se.count
                    v_op("tensor_tensor", gv_grp[:, fl, 512 * u:512 * (u + 1)],
                         sg_bf[:, 512 * u:512 * (u + 1)], bu.ap, OP.mult,
                         waits=W(s_se, bu.done) + [(s_dcons, down_mark[0])],
                         inc=(s_ve, 1))
                    mult_mark[id(bu)] = s_ve.count
                wmarks[gfl] = [(bg.done, bg.done.count) for bg in
                               (bA0, bA1, bB0, bB1)]
            dn_banks = [bC, bD, bE, bA0]
            for fo in range(4):
                idx = 4 * grp + fo
                if fo + 1 < 4:
                    load_wdn_slab(idx + 1)
                slab = wdn_slab[idx % 2][:, :].rearrange(
                    "p (fc j) -> p fc j", j=512)
                for tp in range(2):
                    for fl in range(16):
                        if tp == 0 and fl == 0:
                            p.raw("tensor", (lambda e, _w=slab_ld[idx]:
                                             e.wait_ge(_w[0].h, _w[1])))
                            p.wait("tensor", s_ve)
                        for tloc in range(4):
                            tt_ = 4 * tp + tloc
                            bk_ = dn_banks[tloc]
                            t_op("matmul", bk_.ap[:, 0:512],
                                 gv_grp[:, fl, 128 * tt_:128 * (tt_ + 1)],
                                 slab[:, fl, :],
                                 start=(fl == 0), stop=(fl == 15),
                                 skip_group_check=True,
                                 waits=([(bk_.free,
                                          p.shared.get(("dnf", id(bk_)),
                                                       bk_.free.count)),
                                         (s_se, silu_mark.get(id(bk_), 0))]
                                        if fl == 0 else ()),
                                 inc=((s_dcons, 1) if tloc == 3 else None))
                    dmark = s_dcons.count
                    for tloc in range(4):
                        tt_ = 4 * tp + tloc
                        bk_ = dn_banks[tloc]
                        dsl = dstage[:, 512 * tloc:512 * (tloc + 1)]
                        v_op("tensor_copy", dsl, bk_.ap[:, 0:512],
                             waits=[(s_dcons, dmark),
                                    (s_acc, p.shared.get(("dst", tloc), 0))],
                             inc=(s_ve, 1))
                        v_op("engine_nop", inc=(bk_.free, 1))
                        p.shared[("dnf", id(bk_))] = bk_.free.count
                        p.dma("gpsimd",
                              mlp_acc.ap()[128 * tt_:128 * (tt_ + 1),
                                           512 * fo:512 * (fo + 1)],
                              dsl, s_acc, waits=W(s_ve),
                              accum_op=(OP.bypass if grp == 0 else OP.add))
                        p.shared[("dst", tloc)] = s_acc.count
                slab_rd[idx] = s_dcons.count
            down_mark[0] = s_dcons.count
        barrier()
        sbfree("gT_grp", "win0", "win1", "wup0", "wup1", "wdn0", "wdn1",
               "sg_bf", "dstage", "hT_sb", "b_sb", "mod_scr")

        t2_scr = sball("t2_scr", (128, DM), F32)
        bo8 = [sball(f"bo8_{i}", (128, DM), F32) for i in range(TT)]
        for gi in range(TT):
            buf = xt[gi % 2]
            p.dma("sync", buf[:, :], x2_spill.ap()[128 * gi:128 * (gi + 1), :],
                  s_xt[gi % 2], waits=(W(s_ve) if gi >= 2 else []) + W(s_sp))
            p.dma("sync", bo8[gi][:, :],
                  mlp_acc.ap()[128 * gi:128 * (gi + 1), :], s_cst)
            p.wait("vector", s_xt[gi % 2])
            p.wait("vector", s_cst)
            v_op("tensor_scalar", t2_scr[:, :], buf[:, :],
                 stats_sb[:, gi:gi + 1], stats_sb[:, 8 + gi:8 + gi + 1],
                 OP.mult, OP.add, inc=(s_ve, 1))
            v_op("tensor_tensor", bo8[gi][:, :], bo8[gi][:, :], t2_scr[:, :],
                 OP.add, inc=(s_ve, 1))
            v_op("reduce_sum", SL[:, gi:gi + 1], bo8[gi][:, :], AX,
                 inc=(s_ve, 1))
            sc_op("activation", sq_scr[:, :], bo8[gi][:, :], AF.Square,
                  accum_out=SL[:, 8 + gi:8 + gi + 1], waits=W(s_ve, s_se),
                  inc=(s_se, 1))
        batched_ln_chain(8, EPS_LN)
        for gi in range(TT):
            buf = xt[gi % 2]
            p.dma("sync", buf[:, :], x2_spill.ap()[128 * gi:128 * (gi + 1), :],
                  s_xt[gi % 2], waits=W(s_ve))
            p.wait("vector", s_xt[gi % 2])
            v_op("tensor_scalar", bo8[gi][:, :], bo8[gi][:, :],
                 SL[:, 48 + gi:49 + gi], SL[:, 56 + gi:57 + gi],
                 OP.mult, OP.add, waits=W(s_se, s_ve), inc=(s_ve, 1))
            v_op("tensor_tensor", bo8[gi][:, :], bo8[gi][:, :], buf[:, :],
                 OP.add, inc=(s_ve, 1))
            v_op("engine_nop", inc=(s_ve, 1))
            p.dma("sync", y_out[128 * gi:128 * (gi + 1), :], bo8[gi][:, :],
                  s_ld, waits=W(s_ve))

    p.mark_phase("end")
    build_program.phase_marks = list(p.phase_marks)
    tap_outs = []
    for i, tname in enumerate(taps):
        th = tap_handles[tname]
        shp = list(th.ap().shape)
        dt = th.ap().dtype
        out = nc.dram_tensor(f"tap{i}", shp, dt, kind="ExternalOutput").ap()
        for s in (s_ld, s_gld, s_mm, s_ve, s_se, s_ge, s_cc, s_acc):
            p.wait("sync", s)
        p.dma("sync", out, th.ap(), s_ld)
    p.wait("sync", s_ld)
    p.wait("sync", s_acc)
    p.wait("sync", s_gld)
    p.wait("sync", s_cc)

    with nc.Block() as block:
        @block.tensor
        def _(e):
            for f in p.q["tensor"]:
                f(e)

        @block.vector
        def _(e):
            for f in p.q["vector"]:
                f(e)

        @block.scalar
        def _(e):
            for f in p.q["scalar"]:
                f(e)

        @block.gpsimd
        def _(e):
            for f in p.q["gpsimd"]:
                f(e)

        @block.sync
        def _(e):
            for f in p.q["sync"]:
                f(e)

    return nc


def _make_inputs(inputs):
    f32 = np.float32
    bf = ml_dtypes.bfloat16
    x = np.asarray(inputs["L0_attn_x"], f32).reshape(S, DM)
    WqA = np.asarray(inputs["L0_attn_Wq"], f32)
    WkA = np.asarray(inputs["L0_attn_Wk"], f32)
    WvA = np.asarray(inputs["L0_attn_Wv"], f32)
    bqA = np.asarray(inputs["L0_attn_bq"], f32).reshape(1, DM)
    bkA = np.asarray(inputs["L0_attn_bk"], f32).reshape(1, DM)
    bvA = np.asarray(inputs["L0_attn_bv"], f32).reshape(1, DM)
    WoA = np.asarray(inputs["L0_attn_Wo"], f32)
    boA = np.asarray(inputs["L0_attn_bo"], f32).reshape(1, DM)
    ln1 = np.asarray(inputs["L0_ffn_ln1_w"], f32).reshape(1, DM)
    ln2 = np.asarray(inputs["L0_ffn_ln2_w"], f32).reshape(1, DM)
    WqBT = np.ascontiguousarray(np.asarray(inputs["L0_ffn_Wq"], f32).T)
    WkBT = np.ascontiguousarray(np.asarray(inputs["L0_ffn_Wk"], f32).T)
    WvBT = np.ascontiguousarray(np.asarray(inputs["L0_ffn_Wv"], f32).T)
    WoB = np.asarray(inputs["L0_ffn_Wo"], f32)
    A_lora = np.asarray(inputs["L0_ffn_A"], f32)
    B_lora = np.asarray(inputs["L0_ffn_B"], f32)
    W_inT = np.ascontiguousarray(np.asarray(inputs["L0_ffn_W_in_base"], f32).T)
    W_upT = np.ascontiguousarray(np.asarray(inputs["L0_ffn_W_up_base"], f32).T)
    W_downT = np.ascontiguousarray(np.asarray(inputs["L0_ffn_W_down"], f32).T)

    ident = np.eye(128, dtype=bf)
    trilA = np.triu(np.ones((128, 128), np.float32)).astype(bf)
    maskB = np.zeros((4, 128, 512), np.float32)
    triu = np.triu(np.ones((128, 128), np.float32))
    for j in range(4):
        for t in range(4):
            if t > j:
                maskB[j][:, 128 * t:128 * (t + 1)] = 1.0
            elif t == j:
                maskB[j][:, 128 * t:128 * (t + 1)] = triu
    maskB = maskB.astype(bf)
    ones128 = np.ones((128, 128), dtype=bf)
    ones1f = np.ones((1, 128), dtype=f32)

    in_maps = []
    for c in range(NCORES):
        cols = slice(256 * c, 256 * (c + 1))
        in_maps.append({
            "x_in": np.ascontiguousarray(x[T * c:T * (c + 1)]),
            "cid": np.array([[c]], np.int32),
            "ident": ident, "trilA": trilA, "maskB": maskB,
            "ones128": ones128, "ones1x128f": ones1f,
            "WqA": np.ascontiguousarray(WqA[:, cols]),
            "WkA": np.ascontiguousarray(WkA[:, cols]),
            "WvA": np.ascontiguousarray(WvA[:, cols]),
            "bqA": np.ascontiguousarray(bqA[:, cols]),
            "bkA": np.ascontiguousarray(bkA[:, cols]),
            "bvA": np.ascontiguousarray(bvA[:, cols]),
            "WoA": WoA, "boA": boA, "ln1w": ln1, "ln2w": ln2,
            "WqBT": np.ascontiguousarray(WqBT[:, cols]),
            "WkBT": np.ascontiguousarray(WkBT[:, cols]),
            "WvBT": np.ascontiguousarray(WvBT[:, cols]),
            "WoB": WoB, "A_lora": A_lora, "B_lora": B_lora,
            "W_inT": W_inT, "W_upT": W_upT, "W_downT": W_downT,
        })
    return in_maps


def kernel(**inputs):
    in_maps = _make_inputs(inputs)
    nc = build_program()
    res = run_bass_kernel_spmd(nc, in_maps, core_ids=list(range(NCORES)))
    y = np.concatenate([res.results[c]["y"] for c in range(NCORES)], axis=0)
    return y.reshape(1, S, DM).astype(np.float32)


# version 17 (final)
# speedup vs baseline: 1.0258x; 1.0258x over previous
import numpy as np
import ml_dtypes

import concourse.bass as bass
import concourse.mybir as mybir
from concourse.bass_utils import run_bass_kernel_spmd

F32 = mybir.dt.float32
BF16 = mybir.dt.bfloat16
I32 = mybir.dt.int32
AX = mybir.AxisListType.X
OP = mybir.AluOpType
AF = mybir.ActivationFunctionType

NCORES = 8
S = 8192
DM = 2048
T = S // NCORES
H = 16
HD = 128
HPC = H // NCORES
FF = 8192
R = 16
CH = S // 128
TT = T // 128
FC = DM // 128
NG = S // 512
EPS_LN = 1e-5
EPS_A = 1e-6
EPS_RMS = 1e-6
MOD_SCALE = 0.1
SCALE_B = float(1.0 / np.sqrt(128.0))

ENGINES = ("tensor", "vector", "scalar", "gpsimd", "sync")


class Sem:
    __slots__ = ("h", "count", "name")

    def __init__(self, nc, name):
        self.h = nc.alloc_semaphore(name=name)
        self.count = 0
        self.name = name


class Prog:
    def __init__(self, nc):
        self.nc = nc
        self.q = {e: [] for e in ENGINES}
        self.shared = {}
        self.n_mm = 0
        self.phase_marks = []

    def mark_phase(self, name):
        self.phase_marks.append((name, self.n_mm))

    def sem(self, name):
        return Sem(self.nc, name)

    def raw(self, eng, fn):
        self.q[eng].append(fn)

    def wait(self, eng, sem, val=None):
        v = sem.count if val is None else val
        if v <= 0:
            return
        self.q[eng].append(lambda e, _h=sem.h, _v=v: e.wait_ge(_h, _v))

    def dma(self, eng, out, in_, sem, waits=(), **kw):
        for (ws, wv) in waits:
            self.wait(eng, ws, wv)
        sem.count += 16
        self.q[eng].append(
            lambda e, _o=out, _i=in_, _h=sem.h, _kw=dict(kw): e.dma_start(
                out=_o, in_=_i, **_kw
            ).then_inc(_h, 16)
        )

    def cond_dma8(self, eng, outs_ins, sem, waits=()):
        for (ws, wv) in waits:
            self.wait(eng, ws, wv)
        sem.count += 16

        def f(e, _oi=list(outs_ins), _h=sem.h):
            r = self.shared["cidreg_" + eng]
            for c, (o, i) in enumerate(_oi):
                with e.If_eq(r, c):
                    e.dma_start(out=o, in_=i).then_inc(_h, 16)

        self.q[eng].append(f)

    def op(self, eng, meth, *args, inc=None, waits=(), **kw):
        if eng == "tensor" and meth == "matmul":
            self.n_mm += 1
        if eng == "tensor" and meth == "transpose":
            self.n_tr = getattr(self, "n_tr", 0) + 1
        for (ws, wv) in waits:
            self.wait(eng, ws, wv)
        if inc is not None:
            inc[0].count += inc[1]

        def f(e, _m=meth, _a=args, _kw=dict(kw), _inc=inc):
            ins = getattr(e, _m)(*_a, **_kw)
            if _inc is not None:
                ins.then_inc(_inc[0].h, _inc[1])

        self.q[eng].append(f)

    def collective(self, kind, op, ins, outs, sem, waits=()):
        for (ws, wv) in waits:
            self.wait("gpsimd", ws, wv)
        sem.count += 1

        def f(e, _i=ins, _o=outs, _h=sem.h):
            e.collective_compute(
                kind, op, replica_groups=[list(range(NCORES))], ins=_i, outs=_o
            ).then_inc(_h, 1)

        self.q["gpsimd"].append(f)


class Bank:

    def __init__(self, p, name, ap):
        self.ap = ap
        self.done = p.sem(name + "_d")
        self.free = p.sem(name + "_f")


def build_program(upto=99, taps=()):
    nc = bass.Bass("TRN2", target_bir_lowering=False)

    def din(name, shape, dt=F32):
        return nc.dram_tensor(name, list(shape), dt, kind="ExternalInput").ap()

    x_in = din("x_in", (T, DM))
    cid_in = din("cid", (1, 1), I32)
    ident_in = din("ident", (128, 128), BF16)
    trilA_in = din("trilA", (128, 128), BF16)
    maskB_in = din("maskB", (4, 128, 512), BF16)
    ones_in = din("ones128", (128, 128), BF16)
    ones1f_in = din("ones1x128f", (1, 128), F32)
    wqa_in = din("WqA", (DM, 256))
    wka_in = din("WkA", (DM, 256))
    wva_in = din("WvA", (DM, 256))
    bqa_in = din("bqA", (1, 256))
    bka_in = din("bkA", (1, 256))
    bva_in = din("bvA", (1, 256))
    woa_in = din("WoA", (DM, DM))
    boa_in = din("boA", (1, DM))
    ln1_in = din("ln1w", (1, DM))
    ln2_in = din("ln2w", (1, DM))
    wqb_in = din("WqBT", (DM, 256))
    wkb_in = din("WkBT", (DM, 256))
    wvb_in = din("WvBT", (DM, 256))
    wob_in = din("WoB", (DM, DM))
    alora_in = din("A_lora", (DM, R))
    blora_in = din("B_lora", (R, FF))
    wint_in = din("W_inT", (DM, FF))
    wupt_in = din("W_upT", (DM, FF))
    wdnt_in = din("W_downT", (FF, DM))

    y_out = nc.dram_tensor("y", [T, DM], F32, kind="ExternalOutput").ap()

    def dint(name, shape, dt=BF16):
        return nc.dram_tensor(name, list(shape), dt, kind="Internal")

    def dshared(name, shape):
        return nc.dram_tensor(name, list(shape), BF16, kind="Internal",
                              addr_space="Shared")

    g1_in = [dint(f"g1_in{u}", (DM, 512)) for u in range(2)]
    g1_out = [dshared(f"g1_out{u}", (NCORES * DM, 512)) for u in range(2)]
    g2_in = dint("g2_in", (S, 256))
    g2_out = [dshared(f"g2_out{u}", (NCORES * S // 2, 256)) for u in range(2)]
    g3_in = [dint(f"g3_in{u}", (DM, 512)) for u in range(2)]
    g3_out = [dshared(f"g3_out{u}", (NCORES * DM, 512)) for u in range(2)]
    g4_in = dint("g4_in", (NCORES * 256, T))
    g4_out = [dshared(f"g4_out{u}", (NCORES * 1024, T)) for u in range(2)]
    x2_spill = dint("x2_spill", (T, DM), F32)
    g_dbg = dint("g_dbg", (FF, T), BF16)
    hT_spill = dint("hT_spill", (DM, T))
    mlp_acc = dint("mlp_acc", (T, DM), F32)

    tap_handles = {
        "x2_spill": x2_spill, "hT_spill": hT_spill, "mlp_acc": mlp_acc,
    }

    p = Prog(nc)

    _open = []

    _uid = [0]

    def sball(name, shape, dt):
        _uid[0] += 1
        cm = nc.sbuf_tensor(f"{name}__{_uid[0]}", list(shape), dt)
        h = cm.__enter__()
        _open.append((name, cm))
        return h

    def sbfree(*names):
        want = set(names)
        top = [n for (n, _) in _open[-len(want):]]
        assert set(top) == want, f"non-LIFO sbuf free: top={top} want={sorted(want)}"
        for _ in range(len(want)):
            n, cm = _open.pop()
            cm.__exit__(None, None, None)

    cid_sb = sball("cid_sb", (1, 1), I32)
    ident = sball("ident_sb", (128, 128), BF16)
    trilA = sball("trilA_sb", (128, 128), BF16)
    ones128 = sball("ones128_sb", (128, 128), BF16)
    ones1f = sball("ones1f", (1, 128), F32)
    xt = [sball(f"xt{i}", (128, DM), F32) for i in range(2)]
    sq_scr = sball("sq_scr", (128, DM), F32)
    n1bf = sball("n1bf", (128, DM), BF16)
    n2bf = sball("n2bf", (128, DM), BF16)
    stage = sball("stage", (128, DM), BF16)
    st8 = sball("st8", (128, 8), F32)
    SL = sball("SL", (128, 64), F32)
    stats_sb = sball("stats_sb", (128, 16), F32)
    hA_sb = sball("hA_sb", (128, TT * 16), F32)
    a_sb = sball("a_sb", (128, FC * 16), BF16)
    wcomb = sball("wcomb", (128, FC * 16), BF16)
    modhAT = sball("modhAT", (16, T), BF16)
    negB = sball("negB", (128, 2), F32)
    nrm2 = sball("nrm2", (1, 4), F32)
    att_stage = sball("att_stage", (128, 1024), BF16)
    pv_stage = sball("pv_stage", (128, 512), BF16)
    probs = [sball(f"probs{i}", (128, 1024), BF16) for i in range(2)]
    n1T_cur = [sball(f"n1T{i}", (128, FC * 512), BF16) for i in range(2)]

    ps_a = nc.psum_tensor("ps_a", [128, 1024], F32).__enter__()
    ps_b = nc.psum_tensor("ps_b", [128, 1024], F32).__enter__()
    ps_c = nc.psum_tensor("ps_c", [128, 512], F32).__enter__()
    ps_d = nc.psum_tensor("ps_d", [128, 512], F32).__enter__()
    ps_e = nc.psum_tensor("ps_e", [128, 512], F32).__enter__()
    ps_tr = nc.psum_tensor("ps_tr", [128, 1024], BF16).__enter__()

    bA = Bank(p, "bA", ps_a[:, :])
    bB = Bank(p, "bB", ps_b[:, :])
    bA0 = Bank(p, "bA0", ps_a[:, 0:512])
    bA1 = Bank(p, "bA1", ps_a[:, 512:1024])
    bB0 = Bank(p, "bB0", ps_b[:, 0:512])
    bB1 = Bank(p, "bB1", ps_b[:, 512:1024])
    bC = Bank(p, "bC", ps_c[:, :])
    bD = Bank(p, "bD", ps_d[:, :])
    bE = Bank(p, "bE", ps_e[:, :])
    bT = Bank(p, "bT", ps_tr[:, :])

    s_ld = p.sem("ld")
    s_xt = [p.sem("xt0s"), p.sem("xt1s")]
    s_n1s = [p.sem("n1s0"), p.sem("n1s1")]
    s_ata = p.sem("ata")
    s_g1 = [p.sem("g1a"), p.sem("g1b")]
    s_g2 = [p.sem("g2a"), p.sem("g2b")]
    s_g3 = [p.sem("g3a"), p.sem("g3b")]
    s_g4 = [p.sem("g4a"), p.sem("g4b")]
    s_cst = p.sem("cst")
    s_sp = p.sem("spill")
    s_gld = p.sem("gld")
    s_mm = p.sem("mm")
    s_ve = p.sem("ve")
    s_se = p.sem("se")
    s_ge = p.sem("ge")
    s_cc = p.sem("cc")
    s_acc = p.sem("acc")
    s_slab = p.sem("slab")

    def t_op(meth, *a, **k):
        p.op("tensor", meth, *a, **k)

    def v_op(meth, *a, **k):
        p.op("vector", meth, *a, **k)

    def sc_op(meth, *a, **k):
        p.op("scalar", meth, *a, **k)

    def g_op(meth, *a, **k):
        p.op("gpsimd", meth, *a, **k)

    def W(*sems):
        return [(s, None) for s in sems]

    def barrier():
        t_op("drain", inc=(s_mm, 1))
        v_op("engine_nop", inc=(s_ve, 1))
        sc_op("drain", inc=(s_se, 1))
        g_op("drain", inc=(s_ge, 1))
        allsems = [s_ld, s_gld, s_mm, s_ve, s_se, s_ge, s_acc, s_cst,
                   s_sp, s_ata, s_xt[0], s_xt[1], s_n1s[0], s_n1s[1],
                   s_g1[0], s_g1[1], s_g2[0], s_g2[1], s_g3[0], s_g3[1],
                   s_g4[0], s_g4[1]]
        for eng in ENGINES:
            for s in allsems:
                p.wait(eng, s)

    s_cid = p.sem("cid")
    p.dma("sync", cid_sb[:, :], cid_in[:, :], s_cid)
    p.dma("sync", ident[:, :], ident_in[:, :], s_cst)
    p.dma("sync", trilA[:, :], trilA_in[:, :], s_cst)
    p.dma("sync", ones128[:, :], ones_in[:, :], s_cst)
    p.dma("sync", ones1f[:, :], ones1f_in[:, :], s_cst)
    p.dma("gpsimd", a_sb[:, :].rearrange("p (fc r) -> p fc r", r=16),
          alora_in.rearrange("(fc p) r -> p fc r", p=128), s_gld)

    def _load_cid_sync(e, _ap=cid_sb[0:1, 0:1]):
        r = e.alloc_register("cidreg")
        e.reg_load(r, _ap)
        p.shared["cidreg_sync"] = r

    p.wait("sync", s_cid, 16)
    p.raw("sync", _load_cid_sync)

    def ln_stats(src_f32, eps, inv_out, m2n_out):
        mean, msq, var, tmp, ssq, s1 = (st8[:, i:i + 1] for i in range(6))
        v_op("reduce_sum", s1, src_f32, AX, inc=(s_ve, 1))
        sc_op("activation", sq_scr[:, :], src_f32, AF.Square, accum_out=ssq,
              waits=W(s_ve), inc=(s_se, 1))
        v_op("tensor_scalar", mean, s1, 1.0 / DM, None, OP.mult,
             waits=W(s_ve), inc=(s_ve, 1))
        v_op("tensor_scalar", msq, ssq, 1.0 / DM, None, OP.mult,
             waits=W(s_se), inc=(s_ve, 1))
        sc_op("activation", tmp, mean, AF.Square, waits=W(s_ve), inc=(s_se, 1))
        v_op("tensor_tensor", var, msq, tmp, OP.subtract, waits=W(s_se, s_ve),
             inc=(s_ve, 1))
        v_op("tensor_scalar", var, var, 1.0, eps, OP.mult, OP.add,
             waits=W(s_ve), inc=(s_ve, 1))
        v_op("reciprocal", var, var, waits=W(s_ve), inc=(s_ve, 1))
        sc_op("activation", inv_out, var, AF.Sqrt, waits=W(s_ve), inc=(s_se, 1))
        v_op("tensor_scalar", m2n_out, mean, inv_out, -1.0, OP.mult, OP.mult,
             waits=W(s_se), inc=(s_ve, 1))

    def transpose_tile(dst_col_ap, src_bf, extra_wait=(), sem=None,
                       guards=()):
        sem = s_ld if sem is None else sem
        for half in range(2):
            for k in range(8):
                fcc = half * 8 + k
                t_op("transpose", ps_tr[:, 128 * k:128 * (k + 1)],
                     src_bf[:, 128 * fcc:128 * (fcc + 1)], ident[:, :],
                     waits=(W(bT.free) + list(extra_wait) if k == 0 else ()),
                     inc=((bT.done, 1) if k == 7 else None))
            v_op("tensor_copy", stage[:, 1024 * half:1024 * (half + 1)],
                 ps_tr[:, :], waits=W(bT.done, sem, *guards),
                 inc=(bT.free, 1))
        v_op("engine_nop", inc=(s_ve, 1))
        p.dma("sync", dst_col_ap,
              stage[:, :].rearrange("p (fc j) -> p fc j", j=128),
              sem, waits=W(s_ve))

    g1w = [g1_in[u].ap().rearrange("(fc p) (i j) -> i p fc j",
                                   fc=16, p=128, i=4, j=128) for u in range(2)]
    cc_marks = {}

    wob_cur = [sball(f"wob_cur{i}", (128, FC * 128), BF16) for i in range(2)]
    s_wob = [p.sem("wob0"), p.sem("wob1")]
    wob_mark = {}

    def wcomb_load(kc):
        p.dma("gpsimd",
              wob_cur[kc % 2][:, :].rearrange("p (fc j) -> p fc j", j=128),
              wob_in[:, 128 * kc:128 * (kc + 1)].rearrange(
                  "(fc p) j -> p fc j", p=128),
              s_wob[kc % 2], waits=[(bE.done, wob_mark.get(kc - 2, 0))])
        p.shared[("wcl", kc)] = (s_wob[kc % 2], s_wob[kc % 2].count)

    def wcomb_unit(kc):
        if kc + 1 < FC:
            wcomb_load(kc + 1)
        for jc in range(FC):
            t_op("matmul", ps_e[:, 0:16],
                 wob_cur[kc % 2][:, :].rearrange(
                     "p (fc j) -> p fc j", j=128)[:, jc, :],
                 a_sb[:, :].rearrange("p (fc r) -> p fc r", r=16)[:, jc, :],
                 start=(jc == 0), stop=(jc == FC - 1),
                 waits=(W(bE.free) + [p.shared[("wcl", kc)], (s_gld, 16)]
                        if jc == 0 else ()),
                 inc=((bE.done, 1) if jc == FC - 1 else None))
        wob_mark[kc] = bE.done.count
        v_op("tensor_copy", wcomb[:, 16 * kc:16 * (kc + 1)], ps_e[:, 0:16],
             waits=W(bE.done), inc=(bE.free, 1))

    def batched_ln_chain(w, eps, inv_out=None, m2n_out=None):
        inv_out = SL[:, 48:48 + w] if inv_out is None else inv_out
        m2n_out = SL[:, 56:56 + w] if m2n_out is None else m2n_out
        v_op("tensor_scalar", SL[:, 16:32], SL[:, 0:16], 1.0 / DM,
             None, OP.mult, waits=W(s_ve, s_se), inc=(s_ve, 1))
        sc_op("activation", SL[:, 32:32 + w], SL[:, 16:16 + w], AF.Square,
              waits=W(s_ve), inc=(s_se, 1))
        v_op("scalar_tensor_tensor", SL[:, 40:40 + w], SL[:, 24:24 + w], eps,
             SL[:, 32:32 + w], OP.add, OP.subtract, waits=W(s_se, s_ve),
             inc=(s_ve, 1))
        v_op("reciprocal", SL[:, 40:40 + w], SL[:, 40:40 + w], waits=W(s_ve),
             inc=(s_ve, 1))
        sc_op("activation", inv_out, SL[:, 40:40 + w], AF.Sqrt,
              waits=W(s_ve), inc=(s_se, 1))
        v_op("scalar_tensor_tensor", m2n_out, SL[:, 16:16 + w], -1.0,
             inv_out, OP.mult, OP.mult, waits=W(s_se, s_ve),
             inc=(s_ve, 1))

    p.mark_phase("ph1")
    if upto >= 1:
        wcomb_load(0)
        p.wait("tensor", s_cst)
        p.wait("vector", s_cst)
        for i in range(TT):
            buf = xt[i % 2]
            p.dma("sync", buf[:, :], x_in[128 * i:128 * (i + 1), :],
                  s_xt[i % 2], waits=(W(s_ve, s_se) if i >= 2 else ()))
            p.wait("vector", s_xt[i % 2])
            p.wait("scalar", s_xt[i % 2])
            v_op("reduce_sum", SL[:, i:i + 1], buf[:, :], AX, inc=(s_ve, 1))
            sc_op("activation", sq_scr[:, :], buf[:, :], AF.Square,
                  accum_out=SL[:, 8 + i:8 + i + 1], waits=W(s_se),
                  inc=(s_se, 1))
            if upto >= 6 and i < 4:
                wcomb_unit(2 * i)
                wcomb_unit(2 * i + 1)
        batched_ln_chain(8, EPS_LN)
        for i in range(TT):
            buf = xt[i % 2]
            p.dma("sync", buf[:, :], x_in[128 * i:128 * (i + 1), :],
                  s_xt[i % 2], waits=W(s_ve))
            p.wait("vector", s_xt[i % 2])
            dstb = n1bf if i % 2 == 0 else n2bf
            v_op("tensor_scalar", dstb[:, :], buf[:, :], SL[:, 48 + i:49 + i],
                 SL[:, 56 + i:57 + i], OP.mult, OP.add,
                 waits=W(s_ve, bT.done), inc=(s_ve, 1))
            transpose_tile(g1w[i // 4][i % 4], dstb[:, :], extra_wait=W(s_ve),
                           sem=s_g1[i // 4], guards=(s_g1[1 - i // 4],))
            if upto >= 6 and i < 4:
                wcomb_unit(8 + 2 * i)
                wcomb_unit(8 + 2 * i + 1)
            if i % 4 == 3:
                p.collective("AllGather", OP.bypass,
                             [g1_in[i // 4].ap().opt()],
                             [g1_out[i // 4].ap().opt()], s_cc,
                             waits=W(s_g1[i // 4]))
                cc_marks[("g1", i // 4)] = s_cc.count
        barrier()
        sbfree("wob_cur0", "wob_cur1")

    g1v = [g1_out[u].ap().rearrange("(s fc p) tok -> s p fc tok",
                                    s=8, fc=16, p=128) for u in range(2)]
    g3v = [g3_out[u].ap().rearrange("(s fc p) tok -> s p fc tok",
                                    s=8, fc=16, p=128) for u in range(2)]

    def qk_pass(gv, wq_sb, wk_sb, elu, bqk, qT, kT, ccw):
        qk_banks = [bA0, bA1, bB0, bB1]
        marks = {}
        for tc in range(16):
            buf = n1T_cur[tc % 2]
            p.dma("sync", buf[:, :].rearrange("p (fc tok) -> p fc tok", tok=512),
                  gv[tc % 2][tc // 2], s_n1s[tc % 2],
                  waits=list(marks.get(tc - 2, ())) + [(s_cc, ccw[tc % 2])])
            p.wait("tensor", s_n1s[tc % 2])
            for wi, (wsb, dst) in enumerate(((wq_sb, qT), (wk_sb, kT))):
                for fh in range(2):
                    bk_ = qk_banks[2 * wi + fh]
                    for fcc in range(FC):
                        t_op("matmul", bk_.ap,
                             wsb[:, fcc, 128 * fh:128 * (fh + 1)],
                             buf[:, :].rearrange("p (fc tok) -> p fc tok",
                                                 tok=512)[:, fcc, :],
                             start=(fcc == 0), stop=(fcc == FC - 1),
                             waits=(W(bk_.free) if fcc == 0 else ()),
                             inc=((bk_.done, 1) if fcc == FC - 1 else None))
                    dcol = dst[:, fh * S + 512 * tc: fh * S + 512 * (tc + 1)]
                    if elu:
                        bias = bqk[:, 2 * wi + fh:2 * wi + fh + 1]
                        t1 = sq_scr[:, 0:512]
                        rr = sq_scr[:, 512:1024]
                        ee = sq_scr[:, 1024:1536]
                        v_op("tensor_scalar", t1, bk_.ap, bias, 0.0,
                             OP.add, OP.min, waits=W(bk_.done, s_se),
                             inc=(s_ve, 1))
                        v_op("tensor_scalar", rr, bk_.ap, bias, 0.0,
                             OP.add, OP.max, inc=(bk_.free, 1))
                        sc_op("activation", ee, t1, AF.Exp, waits=W(s_ve),
                              inc=(s_se, 1))
                        v_op("tensor_tensor", dcol, ee, rr, OP.add,
                             waits=W(s_se, bk_.free), inc=(s_ve, 1))
                    else:
                        v_op("tensor_copy", dcol, bk_.ap, waits=W(bk_.done),
                             inc=(bk_.free, 1))
            marks[tc] = [(b.done, b.done.count) for b in qk_banks]

    def v_pass(gv, wv_sb, bias_bc, v_sb, ccw, co_units=None):
        vb = [bC, bD]
        marks = {}
        for tc in range(16):
            buf = n1T_cur[tc % 2]
            p.dma("sync", buf[:, :].rearrange("p (fc tok) -> p fc tok", tok=512),
                  gv[tc % 2][tc // 2], s_n1s[tc % 2],
                  waits=list(marks.get(tc - 2, ())) + [(s_cc, ccw[tc % 2])])
            p.wait("tensor", s_n1s[tc % 2])
            for sub in range(4):
                ch = 4 * tc + sub
                bk_ = vb[sub % 2]
                for fcc in range(FC):
                    t_op("matmul", bk_.ap[:, 0:256],
                         buf[:, :].rearrange("p (fc tok) -> p fc tok",
                                             tok=512)[:, fcc,
                                                      128 * sub:128 * (sub + 1)],
                         wv_sb[:, fcc, :],
                         start=(fcc == 0), stop=(fcc == FC - 1),
                         waits=(W(bk_.free) if fcc == 0 else ()),
                         inc=((bk_.done, 1) if fcc == FC - 1 else None))
                dst = v_sb[:, ch, :, 0:128]
                src = bk_.ap[:, 0:256].rearrange("p (h e) -> p h e", h=2, e=128)
                if bias_bc is not None:
                    v_op("tensor_tensor", dst, src,
                         bias_bc[:, :].rearrange("p (h e) -> p h e", h=2, e=128),
                         OP.add, waits=W(bk_.done), inc=(bk_.free, 1))
                else:
                    v_op("tensor_copy", dst, src, waits=W(bk_.done),
                         inc=(bk_.free, 1))
                if co_units is not None:
                    co_units[4 * tc + sub]()
            marks[tc] = [(b.done, b.done.count) for b in vb]

    p.mark_phase("ph2")
    if upto >= 2:
        qT = sball("qT", (128, 2 * S), BF16)
        kT = sball("kT", (128, 2 * S), BF16)
        v4 = sball("v4", (128, CH * 2 * 132), BF16)
        v_sb = v4[:, :].rearrange("p (ch h e) -> p ch h e", ch=CH, h=2)
        wq_sb = sball("wq_sb", (128, FC * 256), BF16)
        wk_sb = sball("wk_sb", (128, FC * 256), BF16)
        bqk_sb = sball("bqk_sb", (128, 4), F32)

        p.dma("gpsimd", wq_sb[:, :].rearrange("p (fc j) -> p fc j", j=256),
              wqa_in.rearrange("(fc p) j -> p fc j", p=128), s_gld)
        p.dma("gpsimd", wk_sb[:, :].rearrange("p (fc j) -> p fc j", j=256),
              wka_in.rearrange("(fc p) j -> p fc j", p=128), s_gld)
        p.dma("sync", bqk_sb[:, 0:2], bqa_in.rearrange("o (c p) -> (o p) c", p=128),
              s_cst, allow_slow_non_contiguous=True)
        p.dma("sync", bqk_sb[:, 2:4], bka_in.rearrange("o (c p) -> (o p) c", p=128),
              s_cst, allow_slow_non_contiguous=True)
        g_op("memset", v_sb[:, :, :, 128:129], 1.0, inc=(s_ge, 1))
        p.wait("tensor", s_gld)
        p.wait("vector", s_cst)
        qk_pass(g1v, wq_sb[:, :].rearrange("p (fc j) -> p fc j", j=256),
                wk_sb[:, :].rearrange("p (fc j) -> p fc j", j=256),
                True, bqk_sb, qT, kT,
                {0: cc_marks[("g1", 0)], 1: cc_marks[("g1", 1)]})
        barrier()
        wv_sb = wq_sb
        p.dma("gpsimd", wv_sb[:, :].rearrange("p (fc j) -> p fc j", j=256),
              wva_in.rearrange("(fc p) j -> p fc j", p=128), s_gld)
        bv_bc = sball("bv_bc", (128, 256), F32)
        p.dma("sync", bv_bc[:, :], bva_in.broadcast_to((128, 256)), s_cst)
        p.wait("tensor", s_gld)
        p.wait("vector", s_cst)
        v_pass(g1v, wv_sb[:, :].rearrange("p (fc j) -> p fc j", j=256),
               bv_bc, v_sb, {0: cc_marks[("g1", 0)], 1: cc_marks[("g1", 1)]})
        barrier()

    p.mark_phase("ph3")
    if upto >= 3:
        S_ext = sball("S_ext", (128, 2 * 132), F32)
        S_bf = sball("S_bf", (128, 2 * 132), BF16)
        kf_tok = sball("kf_tok", (128, 2 * 128), BF16)
        AT_bf = sball("AT_bf", (128, 2 * 128), BF16)
        g_op("memset", S_ext[:, :], 0.0, inc=(s_ge, 1))
        g_op("memset", S_bf[:, :], 0.0, inc=(s_ge, 1))
        p.wait("tensor", s_ge)
        p.wait("vector", s_ge)
        g2w = g2_in.ap().rearrange("(cb cc p) (h e) -> cb p cc h e",
                                   cb=16, cc=4, p=128, h=2, e=128)
        hbanks = [(bC, bD, bE), (bA0, bA1, bB0)]
        for ch in range(CH):
            for h in range(HPC):
                bAT, bKV, bOUT = hbanks[h]
                Se = S_ext[:, 132 * h:132 * h + 129]
                Sb = S_bf[:, 132 * h:132 * h + 129]
                kft = kf_tok[:, 128 * h:128 * (h + 1)]
                atb = AT_bf[:, 128 * h:128 * (h + 1)]
                qcol = qT[:, h * S + 128 * ch: h * S + 128 * (ch + 1)]
                kcol = kT[:, h * S + 128 * ch: h * S + 128 * (ch + 1)]
                vext = v_sb[:, ch, h, 0:129]
                t_op("transpose", ps_tr[:, 128 * h:128 * (h + 1)], kcol,
                     ident[:, :], waits=W(bT.free), inc=(bT.done, 1))
                v_op("tensor_copy", kft, ps_tr[:, 128 * h:128 * (h + 1)],
                     waits=W(bT.done), inc=(bT.free, 1))
                t_op("matmul", bAT.ap[:, 0:128], kcol, qcol, start=True,
                     stop=True, waits=W(bAT.free, s_ve), inc=(bAT.done, 1))
                v_op("tensor_tensor", atb, bAT.ap[:, 0:128], trilA[:, :],
                     OP.mult, waits=W(bAT.done), inc=(bAT.free, 1))
                v_op("engine_nop", inc=(s_ve, 1))
                t_op("matmul", bOUT.ap[:, 0:129], qcol, Sb, start=True,
                     stop=False, waits=W(bOUT.free, s_ve))
                t_op("matmul", bOUT.ap[:, 0:129], atb, vext, start=False,
                     stop=True, inc=(bOUT.done, 1))
                t_op("matmul", bKV.ap[:, 0:129], kft, vext, start=True,
                     stop=True, waits=W(bKV.free), inc=(bKV.done, 1))
                v_op("tensor_tensor", Se, Se, bKV.ap[:, 0:129], OP.add,
                     waits=W(bKV.done), inc=(bKV.free, 1))
                v_op("tensor_copy", Sb, Se, inc=(s_ve, 1))
                zt = st8[:, h:h + 1]
                v_op("tensor_scalar", zt, bOUT.ap[:, 128:129], 1.0, EPS_A,
                     OP.mult, OP.add, waits=W(bOUT.done), inc=(s_ve, 1))
                v_op("reciprocal", zt, zt, waits=W(s_ve), inc=(s_ve, 1))
                dst = att_stage[:, :].rearrange("p (cc h e) -> p cc h e",
                                                cc=4, h=2, e=128)[:, ch % 4, h, :]
                v_op("tensor_scalar", dst, bOUT.ap[:, 0:128], zt, None, OP.mult,
                     waits=W(s_ld, s_ve), inc=(bOUT.free, 1))
                v_op("engine_nop", inc=(s_ve, 1))
            if ch % 4 == 3:
                p.dma("sync", g2w[ch // 4],
                      att_stage[:, :].rearrange("p (cc h e) -> p cc h e",
                                                cc=4, h=2, e=128),
                      s_g2[ch // 32], waits=W(s_ve))
            if ch == 31:
                p.collective("AllGather", OP.bypass,
                             [g2_in.ap()[0:4096, :].opt()],
                             [g2_out[0].ap().opt()], s_cc, waits=W(s_g2[0]))
                cc_marks[("g2", 0)] = s_cc.count
        p.collective("AllGather", OP.bypass, [g2_in.ap()[4096:8192, :].opt()],
                     [g2_out[1].ap().opt()], s_cc, waits=W(s_g2[1]))
        cc_marks[("g2", 1)] = s_cc.count
        barrier()
        sbfree("S_ext", "S_bf", "kf_tok", "AT_bf")

    p.mark_phase("ph45")
    if upto >= 4:
        sbfree("qT", "kT", "v4", "wq_sb", "wk_sb", "bqk_sb", "bv_bc")
        attnAT = [sball(f"attnAT{i}", (128, DM), BF16) for i in range(4)]
        attnA_t = sball("attnA_t", (128, DM), BF16)
        wo_t = [sball(f"wo_t{i}", (128, DM), F32) for i in range(4)]
        woa_cur = [sball(f"woa_cur{i}", (128, FC * 512), BF16)
                   for i in range(2)]
        bo_bc = sball("bo_bc", (128, DM), F32)
        ln12_bc = sball("ln12_bc", (128, 2 * DM), F32)
        t_scr = sball("t_scr", (128, DM), F32)

        p.dma("sync", bo_bc[:, :], boa_in.broadcast_to((128, DM)), s_cst)
        p.dma("sync", ln12_bc[:, 0:DM], ln1_in.broadcast_to((128, DM)), s_cst)
        p.dma("sync", ln12_bc[:, DM:2 * DM], ln2_in.broadcast_to((128, DM)),
              s_cst)
        p.wait("vector", s_cst)

        g2v = [g2_out[u].ap().rearrange("(s c i p) (h e) -> c i p s h e",
                                        s=8, c=4, i=8, p=128, h=2, e=128)
               for u in range(2)]
        g3w = [g3_in[u].ap().rearrange("(fc p) (i j) -> i p fc j",
                                       fc=16, p=128, i=4, j=128)
               for u in range(2)]
        hTw = hT_spill.ap().rearrange("(fc p) (i j) -> i p fc j", fc=16, p=128, i=8, j=128)
        woav = [woa_cur[i][:, :].rearrange("p (fc j) -> p fc j", j=512)
                for i in range(2)]
        wo_banks = [bA0, bA1, bB0, bB1]
        woa_marks = {}

        s_woa = [p.sem("woa0"), p.sem("woa1")]

        def load_woa(th, fo):
            k = 4 * th + fo
            p.dma("gpsimd", woav[k % 2],
                  woa_in[:, 512 * fo:512 * (fo + 1)].rearrange(
                      "(fc p) j -> p fc j", p=128),
                  s_woa[k % 2], waits=woa_marks.get(k - 2, ()))
            p.shared[("woal", k)] = (s_woa[k % 2], s_woa[k % 2].count)

        att_free = {}
        for th in range(2):
            for i in range(4):
                gi = 4 * th + i
                p.cond_dma8(
                    "sync",
                    [(attnA_t[:, :].rearrange("p (s h e) -> p s h e", s=8, h=2, e=128),
                      g2v[c // 4][c % 4, gi]) for c in range(8)],
                    s_ata,
                    waits=W(bT.done) + [(s_cc, cc_marks[("g2", 1)])])
                for half in range(2):
                    for k in range(8):
                        fcc = half * 8 + k
                        t_op("transpose", ps_tr[:, 128 * k:128 * (k + 1)],
                             attnA_t[:, 128 * fcc:128 * (fcc + 1)], ident[:, :],
                             waits=(W(bT.free, s_ata) if k == 0 else ()),
                             inc=((bT.done, 1) if k == 7 else None))
                    v_op("tensor_copy", attnAT[i][:, 1024 * half:1024 * (half + 1)],
                         ps_tr[:, :],
                         waits=W(bT.done) + att_free.get(i, []),
                         inc=(bT.free, 1))
                v_op("engine_nop", inc=(s_ve, 1))
            if th == 0:
                load_woa(0, 0)
            for fo in range(4):
                k = 4 * th + fo
                if fo + 1 < 4:
                    load_woa(th, fo + 1)
                elif th == 0:
                    load_woa(1, 0)
                p.raw("tensor", (lambda e, _w=p.shared[("woal", k)]:
                                 e.wait_ge(_w[0].h, _w[1])))
                p.wait("tensor", s_ve)
                for i in range(4):
                    bk_ = wo_banks[i]
                    for fcc in range(FC):
                        t_op("matmul", bk_.ap,
                             attnAT[i][:, 128 * fcc:128 * (fcc + 1)],
                             woav[k % 2][:, fcc, :],
                             start=(fcc == 0), stop=(fcc == FC - 1),
                             waits=(W(bk_.free) if fcc == 0 else ()),
                             inc=((bk_.done, 1) if fcc == FC - 1 else None))
                    v_op("tensor_tensor", wo_t[i][:, 512 * fo:512 * (fo + 1)],
                         bk_.ap, bo_bc[:, 512 * fo:512 * (fo + 1)], OP.add,
                         waits=W(bk_.done), inc=(bk_.free, 1))
                woa_marks[k] = [(b.done, b.done.count) for b in wo_banks]
                if fo == 3:
                    for i2 in range(4):
                        att_free[i2] = [(wo_banks[i2].done,
                                         wo_banks[i2].done.count)]
            for i in range(4):
                v_op("reduce_sum", SL[:, i:i + 1], wo_t[i][:, :], AX,
                     inc=(s_ve, 1))
                sc_op("activation", sq_scr[:, :], wo_t[i][:, :], AF.Square,
                      accum_out=SL[:, 8 + i:8 + i + 1], waits=W(s_ve, s_se),
                      inc=(s_se, 1))
            batched_ln_chain(4, EPS_LN)
            for i in range(4):
                gi = 4 * th + i
                buf = xt[i % 2]
                p.dma("sync", buf[:, :], x_in[128 * gi:128 * (gi + 1), :],
                      s_xt[i % 2], waits=W(s_ve))
                p.wait("vector", s_xt[i % 2])
                v_op("tensor_scalar", wo_t[i][:, :], wo_t[i][:, :],
                     SL[:, 48 + i:49 + i], SL[:, 56 + i:57 + i],
                     OP.mult, OP.add, waits=W(s_se, s_ve), inc=(s_ve, 1))
                v_op("tensor_tensor", wo_t[i][:, :], wo_t[i][:, :], buf[:, :],
                     OP.add, inc=(s_ve, 1))
                v_op("engine_nop", inc=(s_ve, 1))
                p.dma("sync", x2_spill.ap()[128 * gi:128 * (gi + 1), :],
                      wo_t[i][:, :], s_sp, waits=W(s_ve))
                v_op("reduce_sum", SL[:, i:i + 1], wo_t[i][:, :], AX,
                     inc=(s_ve, 1))
                sc_op("activation", sq_scr[:, :], wo_t[i][:, :], AF.Square,
                      accum_out=SL[:, 8 + i:8 + i + 1], waits=W(s_ve, s_se),
                      inc=(s_se, 1))
            batched_ln_chain(4, EPS_LN,
                             inv_out=stats_sb[:, 4 * th:4 * th + 4],
                             m2n_out=stats_sb[:, 8 + 4 * th:12 + 4 * th])
            sc_op("activation", SL[:, 32:36], stats_sb[:, 4 * th:4 * th + 4],
                  AF.Square, waits=W(s_ve), inc=(s_se, 1))
            v_op("tensor_scalar", SL[:, 32:36], SL[:, 32:36], -EPS_LN,
                 1.0 + EPS_RMS, OP.mult, OP.add, waits=W(s_se), inc=(s_ve, 1))
            v_op("reciprocal", SL[:, 32:36], SL[:, 32:36], waits=W(s_ve),
                 inc=(s_ve, 1))
            sc_op("activation", SL[:, 36:40], SL[:, 32:36], AF.Sqrt,
                  waits=W(s_ve), inc=(s_se, 1))
            for i in range(4):
                gi = 4 * th + i
                rmsv = SL[:, 36 + i:37 + i]
                v_op("tensor_scalar", wo_t[i][:, :], wo_t[i][:, :],
                     stats_sb[:, gi:gi + 1], stats_sb[:, 8 + gi:8 + gi + 1],
                     OP.mult, OP.add, waits=W(s_se, s_ve), inc=(s_ve, 1))
                v_op("scalar_tensor_tensor", n1bf[:, :], wo_t[i][:, :], rmsv,
                     ln12_bc[:, 0:DM], OP.mult, OP.mult,
                     waits=W(bT.done), inc=(s_ve, 1))
                v_op("scalar_tensor_tensor", n2bf[:, :], wo_t[i][:, :], rmsv,
                     ln12_bc[:, DM:2 * DM], OP.mult, OP.mult, inc=(s_ve, 1))
                transpose_tile(g3w[gi // 4][gi % 4], n1bf[:, :],
                               extra_wait=W(s_ve), sem=s_g3[gi // 4],
                               guards=(s_g3[1 - gi // 4], s_sp))
                for half in range(2):
                    for k in range(8):
                        fcc = half * 8 + k
                        t_op("transpose", ps_tr[:, 128 * k:128 * (k + 1)],
                             n2bf[:, 128 * fcc:128 * (fcc + 1)], ident[:, :],
                             waits=(W(bT.free, s_ve) if k == 0 else ()),
                             inc=((bT.done, 1) if k == 7 else None))
                    v_op("tensor_copy", stage[:, 1024 * half:1024 * (half + 1)],
                         ps_tr[:, :],
                         waits=W(bT.done, s_sp, s_g3[0], s_g3[1]),
                         inc=(bT.free, 1))
                v_op("engine_nop", inc=(s_ve, 1))
                p.dma("sync", hTw[gi],
                      stage[:, :].rearrange("p (fc j) -> p fc j", j=128),
                      s_sp, waits=W(s_ve))
                for fcc in range(FC):
                    t_op("matmul", ps_e[:, 0:16],
                         stage[:, 128 * fcc:128 * (fcc + 1)],
                         a_sb[:, :].rearrange("p (fc r) -> p fc r", r=16)[:, fcc, :],
                         start=(fcc == 0), stop=(fcc == FC - 1),
                         waits=(W(bE.free, s_ve) if fcc == 0 else ()),
                         inc=((bE.done, 1) if fcc == FC - 1 else None))
                v_op("tensor_copy", hA_sb[:, 16 * gi:16 * (gi + 1)],
                     ps_e[:, 0:16], waits=W(bE.done), inc=(bE.free, 1))
            if th == 0:
                p.collective("AllGather", OP.bypass, [g3_in[0].ap().opt()],
                             [g3_out[0].ap().opt()], s_cc, waits=W(s_g3[0]))
                cc_marks[("g3", 0)] = s_cc.count
        p.collective("AllGather", OP.bypass, [g3_in[1].ap().opt()],
                     [g3_out[1].ap().opt()], s_cc, waits=W(s_g3[1]))
        cc_marks[("g3", 1)] = s_cc.count
        barrier()
        sbfree("attnAT0", "attnAT1", "attnAT2", "attnAT3", "attnA_t",
               "wo_t0", "wo_t1", "wo_t2", "wo_t3", "woa_cur0", "woa_cur1", "bo_bc",
               "ln12_bc", "t_scr")

    p.mark_phase("ph6")
    if upto >= 6:
        wob_cur = [sball(f"wob_cur{i}", (128, FC * 128), BF16)
                   for i in range(2)]
        wob_mark = {}
        for kc in range(FC):
            p.dma("gpsimd",
                  wob_cur[kc % 2][:, :].rearrange("p (fc j) -> p fc j", j=128),
                  wob_in[:, 128 * kc:128 * (kc + 1)].rearrange(
                      "(fc p) j -> p fc j", p=128),
                  s_gld, waits=[(bE.done, wob_mark.get(kc - 2, 0))])
            gmark = s_gld.count
            for jc in range(FC):
                t_op("matmul", ps_e[:, 0:16],
                     wob_cur[kc % 2][:, :].rearrange(
                         "p (fc j) -> p fc j", j=128)[:, jc, :],
                     a_sb[:, :].rearrange("p (fc r) -> p fc r", r=16)[:, jc, :],
                     start=(jc == 0), stop=(jc == FC - 1),
                     waits=(W(bE.free) + [(s_gld, gmark)] if jc == 0 else ()),
                     inc=((bE.done, 1) if jc == FC - 1 else None))
            wob_mark[kc] = bE.done.count
            v_op("tensor_copy", wcomb[:, 16 * kc:16 * (kc + 1)], ps_e[:, 0:16],
                 waits=W(bE.done), inc=(bE.free, 1))
        barrier()
        sbfree("wob_cur0", "wob_cur1")
        qT = sball("qT", (128, 2 * S), BF16)
        kT = sball("kT", (128, 2 * S), BF16)
        v4 = sball("v4", (128, CH * 2 * 132), BF16)
        v_sb = v4[:, :].rearrange("p (ch h e) -> p ch h e", ch=CH, h=2)
        g_op("memset", v_sb[:, :, :, 128:129], 1.0, inc=(s_ge, 1))
        p.wait("vector", s_ge)
        wq_sb = sball("wq_sb", (128, FC * 256), BF16)
        wk_sb = sball("wk_sb", (128, FC * 256), BF16)
        p.dma("gpsimd", wq_sb[:, :].rearrange("p (fc j) -> p fc j", j=256),
              wqb_in.rearrange("(fc p) j -> p fc j", p=128), s_gld)
        p.dma("gpsimd", wk_sb[:, :].rearrange("p (fc j) -> p fc j", j=256),
              wkb_in.rearrange("(fc p) j -> p fc j", p=128), s_gld)
        p.wait("tensor", s_gld)
        qk_pass(g3v, wq_sb[:, :].rearrange("p (fc j) -> p fc j", j=256),
                wk_sb[:, :].rearrange("p (fc j) -> p fc j", j=256),
                False, None, qT, kT,
                {0: cc_marks[("g3", 0)], 1: cc_marks[("g3", 1)]})
        barrier()
        wv_sb = wq_sb
        p.dma("gpsimd", wv_sb[:, :].rearrange("p (fc j) -> p fc j", j=256),
              wvb_in.rearrange("(fc p) j -> p fc j", p=128), s_gld)
        p.wait("tensor", s_gld)
        knscr = sball("knscr", (128, 512), BF16)
        g_op("memset", nrm2[:, :], 0.0, inc=(s_ge, 1))
        p.wait("vector", s_ge)
        p.wait("scalar", s_ge)

        def knorm_unit(ti, h, tc):
            tens = (qT, kT)[ti]
            col = tens[:, h * S + 512 * tc: h * S + 512 * (tc + 1)]
            sc_op("activation", knscr[:, :], col, AF.Square,
                  waits=W(s_ve, s_se, bE.done), inc=(s_se, 1))
            t_op("matmul", ps_e[0:1, 0:512], ones128[:, 0:1], knscr[:, :],
                 start=True, stop=True, waits=W(s_se, bE.free),
                 inc=(bE.done, 1))
            v_op("reduce_max", st8[0:1, 0:1], ps_e[0:1, 0:512], AX,
                 waits=W(bE.done), inc=(bE.free, 1))
            v_op("engine_nop", inc=(s_ve, 1))
            v_op("tensor_tensor", nrm2[0:1, 2 * ti + h:2 * ti + h + 1],
                 nrm2[0:1, 2 * ti + h:2 * ti + h + 1], st8[0:1, 0:1],
                 OP.max, waits=W(s_ve), inc=(s_ve, 1))

        units = [(lambda _t=ti, _h=h, _c=tc: knorm_unit(_t, _h, _c))
                 for ti in range(2) for h in range(HPC) for tc in range(16)]
        v_pass(g3v, wv_sb[:, :].rearrange("p (fc j) -> p fc j", j=256),
               None, v_sb,
               {0: cc_marks[("g3", 0)], 1: cc_marks[("g3", 1)]},
               co_units=units)
        barrier()
        v_op("tensor_tensor", nrm2[0:1, 0:2], nrm2[0:1, 0:2], nrm2[0:1, 2:4],
             OP.mult, waits=W(s_ve), inc=(s_ve, 1))
        sc_op("activation", nrm2[0:1, 0:2], nrm2[0:1, 0:2], AF.Sqrt,
              waits=W(s_ve), inc=(s_se, 1))
        v_op("tensor_scalar", nrm2[0:1, 0:2], nrm2[0:1, 0:2], -SCALE_B, None,
             OP.mult, waits=W(s_se), inc=(s_ve, 1))
        p.wait("tensor", s_ve)
        t_op("matmul", ps_e[:, 0:2], ones1f[:, :], nrm2[0:1, 0:2],
             start=True, stop=True, waits=W(bE.free), inc=(bE.done, 1))
        v_op("tensor_copy", negB[:, 0:2], ps_e[:, 0:2], waits=W(bE.done),
             inc=(bE.free, 1))
        barrier()
        sbfree("knscr")

    p.mark_phase("ph7")
    if upto >= 7:
        maskB_sb = sball("maskB_sb", (128, 4 * 512), BF16)
        rb_sb = sball("rb_sb", (128, 512), F32)
        p.dma("sync", maskB_sb[:, :].rearrange("p (m n) -> p m n", m=4),
              maskB_in.rearrange("m p n -> p m n"), s_cst)
        recip_sb = sball("recip_sb", (1, 512), F32)
        g4wv = g4_in.ap().rearrange("(h sh e) (u q) -> sh h u e q",
                                    h=2, sh=8, e=128, u=2, q=512)
        p.wait("vector", s_cst)
        p.wait("scalar", s_ge)
        sc_banks = [bA, bB]
        for h in range(HPC):
            for g in range(NG):
                qcols = qT[:, h * S + 512 * g: h * S + 512 * (g + 1)]
                npair = 2 * g + 2
                ready_se = {}
                ready_ve = {}
                for kbp in range(npair + 1):
                    if kbp < npair:
                        bk_ = sc_banks[kbp % 2]
                        for half in range(2):
                            kb = 2 * kbp + half
                            t_op("matmul", bk_.ap[:, 512 * half:512 * (half + 1)],
                                 kT[:, h * S + 128 * kb: h * S + 128 * (kb + 1)],
                                 qcols, start=True, stop=True,
                                 waits=([(s_se, p.shared.get(
                                     ("expmark", kbp % 2), 0))]
                                        if half == 0 else ()),
                                 inc=((bk_.done, 1) if half == 1 else None))
                        slab = probs[kbp % 2]
                        sc_op("activation", slab[:, :], bk_.ap, AF.Exp,
                              bias=negB[:, h:h + 1], scale=SCALE_B,
                              waits=W(bk_.done, s_slab), inc=(s_se, 1))
                        p.shared[("expmark", kbp % 2)] = s_se.count
                        ready_se[kbp] = s_se.count
                        kb0 = 2 * kbp
                        for half in range(2):
                            kb = kb0 + half
                            if 4 * g <= kb <= 4 * g + 3:
                                j = kb - 4 * g
                                v_op("tensor_tensor",
                                     slab[:, 512 * half:512 * (half + 1)],
                                     slab[:, 512 * half:512 * (half + 1)],
                                     maskB_sb[:, 512 * j:512 * (j + 1)],
                                     OP.mult,
                                     waits=[(s_se, ready_se[kbp])],
                                     inc=(s_ve, 1))
                                ready_ve[kbp] = s_ve.count
                    if kbp >= 1:
                        pk = kbp - 1
                        slab = probs[pk % 2]
                        if pk in ready_ve:
                            wl = [(s_ve, ready_ve[pk])]
                        else:
                            wl = [(s_se, ready_se[pk])]
                        for half in range(2):
                            kb = 2 * pk + half
                            t_op("matmul", ps_c[:, 0:512],
                                 v_sb[:, kb, h, 0:128],
                                 slab[:, 512 * half:512 * (half + 1)],
                                 start=(kb == 0), stop=(kb == npair * 2 - 1),
                                 skip_group_check=True,
                                 waits=(wl if half == 0 else ()))
                            t_op("matmul", ps_d[:, 0:512],
                                 ones128[:, :],
                                 slab[:, 512 * half:512 * (half + 1)],
                                 start=(kb == 0), stop=(kb == npair * 2 - 1),
                                 skip_group_check=True,
                                 inc=((s_slab, 1) if half == 1 else None))
                v_op("tensor_scalar", recip_sb[:, :], ps_d[0:1, 0:512], 1.0,
                     1e-30, OP.mult, OP.add, waits=W(s_slab), inc=(s_ve, 1))
                v_op("reciprocal", recip_sb[:, :], recip_sb[:, :],
                     waits=W(s_ve), inc=(s_ve, 1))
                p.wait("tensor", s_ve)
                t_op("matmul", ps_e[:, 0:512], ones1f[:, :], recip_sb[:, :],
                     start=True, stop=True, waits=W(bE.free), inc=(bE.done, 1))
                v_op("tensor_copy", rb_sb[:, :], ps_e[:, 0:512],
                     waits=W(bE.done), inc=(bE.free, 1))
                v_op("tensor_tensor", pv_stage[:, :], ps_c[:, 0:512],
                     rb_sb[:, :], OP.mult, waits=W(s_g4[0], s_g4[1]),
                     inc=(s_ve, 1))
                p.dma("sync", g4wv[g // 2, h, g % 2], pv_stage[:, :],
                      s_g4[h], waits=W(s_ve))
                p.wait("tensor", s_ve)
            if h == 0:
                p.collective("AllGather", OP.bypass,
                             [g4_in.ap()[0:1024, :].opt()],
                             [g4_out[0].ap().opt()], s_cc, waits=W(s_g4[0]))
                cc_marks[("g4", 0)] = s_cc.count
        p.collective("AllGather", OP.bypass, [g4_in.ap()[1024:2048, :].opt()],
                     [g4_out[1].ap().opt()], s_cc, waits=W(s_g4[1]))
        cc_marks[("g4", 1)] = s_cc.count
        barrier()
        sbfree("maskB_sb", "rb_sb", "recip_sb")

    p.mark_phase("ph8")
    if upto >= 8:
        sbfree("wq_sb", "wk_sb", "qT", "kT", "v4")
        hT_sb = sball("hT_sb", (128, FC * T), BF16)
        b_sb = sball("b_sb", (16, DM), BF16)
        mod_scr = sball("mod_scr", (128, 16), F32)
        attnBT = [sball(f"attnBT{i}", (128, DM), BF16) for i in range(2)]
        g4v = [g4_out[u].ap().rearrange("(s c e) (ti tok) -> c ti e s tok",
                                        s=8, c=8, e=128, ti=8, tok=128)
               for u in range(2)]

        s_bg = p.sem("bgld")
        s_hTl = p.sem("hTl")
        p.dma("sync", hT_sb[:, :].rearrange("p (fc t) -> p fc t", t=T),
              hT_spill.ap().rearrange("(fc p) t -> p fc t", fc=16, p=128),
              s_hTl)
        mod_done = {}

        s_abt = [p.sem("abt0"), p.sem("abt1")]

        def read_attnBT(gi):
            for u in range(2):
                p.cond_dma8(
                    "sync",
                    [(attnBT[gi % 2][:, :].rearrange(
                        "p (s h tok) -> p s h tok",
                        s=8, h=2, tok=128)[:, :, u, :],
                      g4v[u][c, gi]) for c in range(8)],
                    s_abt[gi % 2],
                    waits=[(bE.done, mod_done.get(gi - 2, 0)),
                           (s_cc, cc_marks[("g4", u)])])

        read_attnBT(0)
        for gi in range(TT):
            if gi + 1 < TT:
                read_attnBT(gi + 1)
            p.wait("tensor", s_abt[gi % 2])
            for fcc in range(FC):
                t_op("matmul", ps_e[:, 0:16],
                     attnBT[gi % 2][:, 128 * fcc:128 * (fcc + 1)],
                     wcomb[:, 16 * fcc:16 * (fcc + 1)],
                     start=(fcc == 0), stop=(fcc == FC - 1),
                     waits=(W(bE.free) if fcc == 0 else ()),
                     inc=((bE.done, 1) if fcc == FC - 1 else None))
            mod_done[gi] = bE.done.count
            sc_op("activation", mod_scr[:, :], ps_e[:, 0:16], AF.Tanh,
                  waits=W(bE.done), inc=(s_se, 1))
            v_op("tensor_scalar", mod_scr[:, :], mod_scr[:, :], MOD_SCALE, None,
                 OP.mult, waits=W(s_se), inc=(s_ve, 1))
            v_op("engine_nop", inc=(bE.free, 1))
            v_op("tensor_tensor", mod_scr[:, :], mod_scr[:, :],
                 hA_sb[:, 16 * gi:16 * (gi + 1)], OP.mult, waits=W(s_ve),
                 inc=(s_ve, 1))
            v_op("tensor_copy", n1bf[:, 0:16], mod_scr[:, :], waits=W(s_ve),
                 inc=(s_ve, 1))
            p.wait("tensor", s_ve)
            t_op("transpose", ps_tr[0:16, 0:128], n1bf[:, 0:16], ident[:, :],
                 waits=W(bT.free), inc=(bT.done, 1))
            v_op("tensor_copy", modhAT[0:16, 128 * gi:128 * (gi + 1)],
                 ps_tr[0:16, 0:128], waits=W(bT.done), inc=(bT.free, 1))
        barrier()
        sbfree("attnBT0", "attnBT1")

        gT_grp = sball("gT_grp", (128, 16 * T), BF16)
        win_cur = [sball(f"win{i}", (128, FC * 128), BF16) for i in range(2)]
        wup_cur = [sball(f"wup{i}", (128, FC * 128), BF16) for i in range(2)]
        wdn_slab = [sball(f"wdn{i}", (128, 16 * 512), BF16) for i in range(2)]
        sg_bf = sball("sg_bf", (128, 1024), BF16)
        dstage = sball("dstage", (128, 4 * 512), F32)
        gv_grp = gT_grp[:, :].rearrange("p (fc t) -> p fc t", t=T)
        hTv = hT_sb[:, :].rearrange("p (fc t) -> p fc t", t=T)

        silu_mark = {}
        mult_mark = {}
        wmarks = {}
        s_wio = [p.sem("wio0"), p.sem("wio1")]
        s_dcons = p.sem("dcons")
        down_mark = [0]
        slab_rd = {}
        slab_ld = {}

        s_wdn = [p.sem("wdn0"), p.sem("wdn1")]

        def load_wdn_slab(idx):
            grp_, fo_ = divmod(idx, 4)
            p.dma("gpsimd",
                  wdn_slab[idx % 2][:, :].rearrange("p (fc j) -> p fc j", j=512),
                  wdnt_in[2048 * grp_:2048 * (grp_ + 1),
                          512 * fo_:512 * (fo_ + 1)].rearrange(
                      "(fc p) j -> p fc j", p=128),
                  s_wdn[idx % 2], waits=[(s_dcons, slab_rd.get(idx - 2, 0))])
            slab_ld[idx] = (s_wdn[idx % 2], s_wdn[idx % 2].count)
        for grp in range(4):
            load_wdn_slab(4 * grp)
            p.dma("gpsimd", b_sb[:, :],
                  blora_in[:, 2048 * grp:2048 * (grp + 1)], s_bg,
                  waits=[(s_se, silu_mark.get(id(bA1), 0))])
            for fl in range(16):
                gfl = 16 * grp + fl
                Fcg = 16 * grp + fl
                wi = win_cur[fl % 2]
                wu = wup_cur[fl % 2]
                sw = s_wio[fl % 2]
                p.dma("gpsimd", wi[:, :].rearrange("p (dc j) -> p dc j", j=128),
                      wint_in[:, 128 * Fcg:128 * (Fcg + 1)].rearrange(
                          "(dc p) j -> p dc j", p=128),
                      sw, waits=wmarks.get(gfl - 2, ()))
                p.dma("gpsimd", wu[:, :].rearrange("p (dc j) -> p dc j", j=128),
                      wupt_in[:, 128 * Fcg:128 * (Fcg + 1)].rearrange(
                          "(dc p) j -> p dc j", p=128), sw)
                p.wait("tensor", sw)
                p.wait("tensor", s_bg)
                p.wait("tensor", s_hTl)
                for u in range(2):
                    bg = [bA0, bA1][u]
                    bu = [bB0, bB1][u]
                    for dc in range(FC):
                        t_op("matmul", bg.ap,
                             wi[:, :].rearrange("p (dc j) -> p dc j",
                                                j=128)[:, dc, :],
                             hTv[:, dc, 512 * u:512 * (u + 1)],
                             start=(dc == 0), stop=False,
                             waits=([(s_se, silu_mark.get(id(bg), 0))]
                                    + W(bg.free) if dc == 0 else ()))
                    t_op("matmul", bg.ap, b_sb[:, 128 * fl:128 * (fl + 1)],
                         modhAT[:, 512 * u:512 * (u + 1)],
                         start=False, stop=True, inc=(bg.done, 1))
                    for dc in range(FC):
                        t_op("matmul", bu.ap,
                             wu[:, :].rearrange("p (dc j) -> p dc j",
                                                j=128)[:, dc, :],
                             hTv[:, dc, 512 * u:512 * (u + 1)],
                             start=(dc == 0), stop=(dc == FC - 1),
                             waits=([(s_ve, mult_mark.get(id(bu), 0))]
                                    if dc == 0 else ()),
                             inc=((bu.done, 1) if dc == FC - 1 else None))
                    sc_op("activation", sg_bf[:, 512 * u:512 * (u + 1)], bg.ap,
                          AF.Silu,
                          waits=W(bg.done) + [(s_ve,
                                               mult_mark.get(id(bu), 0))],
                          inc=(s_se, 1))
                    silu_mark[id(bg)] = s_se.count
                    v_op("tensor_tensor", gv_grp[:, fl, 512 * u:512 * (u + 1)],
                         sg_bf[:, 512 * u:512 * (u + 1)], bu.ap, OP.mult,
                         waits=W(s_se, bu.done) + [(s_dcons, down_mark[0])],
                         inc=(s_ve, 1))
                    mult_mark[id(bu)] = s_ve.count
                wmarks[gfl] = [(bg.done, bg.done.count) for bg in
                               (bA0, bA1, bB0, bB1)]
            dn_banks = [bC, bD, bE, bA0]
            for fo in range(4):
                idx = 4 * grp + fo
                if fo + 1 < 4:
                    load_wdn_slab(idx + 1)
                slab = wdn_slab[idx % 2][:, :].rearrange(
                    "p (fc j) -> p fc j", j=512)
                for tp in range(2):
                    for fl in range(16):
                        if tp == 0 and fl == 0:
                            p.raw("tensor", (lambda e, _w=slab_ld[idx]:
                                             e.wait_ge(_w[0].h, _w[1])))
                            p.wait("tensor", s_ve)
                        for tloc in range(4):
                            tt_ = 4 * tp + tloc
                            bk_ = dn_banks[tloc]
                            t_op("matmul", bk_.ap[:, 0:512],
                                 gv_grp[:, fl, 128 * tt_:128 * (tt_ + 1)],
                                 slab[:, fl, :],
                                 start=(fl == 0), stop=(fl == 15),
                                 skip_group_check=True,
                                 waits=([(bk_.free,
                                          p.shared.get(("dnf", id(bk_)),
                                                       bk_.free.count)),
                                         (s_se, silu_mark.get(id(bk_), 0))]
                                        if fl == 0 else ()),
                                 inc=((s_dcons, 1) if tloc == 3 else None))
                    dmark = s_dcons.count
                    for tloc in range(4):
                        tt_ = 4 * tp + tloc
                        bk_ = dn_banks[tloc]
                        dsl = dstage[:, 512 * tloc:512 * (tloc + 1)]
                        v_op("tensor_copy", dsl, bk_.ap[:, 0:512],
                             waits=[(s_dcons, dmark),
                                    (s_acc, p.shared.get(("dst", tloc), 0))],
                             inc=(s_ve, 1))
                        v_op("engine_nop", inc=(bk_.free, 1))
                        p.shared[("dnf", id(bk_))] = bk_.free.count
                        p.dma("gpsimd",
                              mlp_acc.ap()[128 * tt_:128 * (tt_ + 1),
                                           512 * fo:512 * (fo + 1)],
                              dsl, s_acc, waits=W(s_ve),
                              accum_op=(OP.bypass if grp == 0 else OP.add))
                        p.shared[("dst", tloc)] = s_acc.count
                slab_rd[idx] = s_dcons.count
            down_mark[0] = s_dcons.count
        barrier()
        sbfree("gT_grp", "win0", "win1", "wup0", "wup1", "wdn0", "wdn1",
               "sg_bf", "dstage", "hT_sb", "b_sb", "mod_scr")

        t2_scr = sball("t2_scr", (128, DM), F32)
        for gi in range(TT):
            buf = xt[gi % 2]
            p.dma("sync", buf[:, :], x2_spill.ap()[128 * gi:128 * (gi + 1), :],
                  s_xt[gi % 2], waits=(W(s_ve) if gi >= 2 else []) + W(s_sp))
            p.dma("sync", sq_scr[:, :],
                  mlp_acc.ap()[128 * gi:128 * (gi + 1), :], s_cst,
                  waits=(W(s_se, s_ve) if gi >= 1 else ()))
            p.wait("vector", s_xt[gi % 2])
            p.wait("vector", s_cst)
            v_op("tensor_scalar", t2_scr[:, :], buf[:, :],
                 stats_sb[:, gi:gi + 1], stats_sb[:, 8 + gi:8 + gi + 1],
                 OP.mult, OP.add, inc=(s_ve, 1))
            v_op("tensor_tensor", t2_scr[:, :], t2_scr[:, :], sq_scr[:, :],
                 OP.add, inc=(s_ve, 1))
            p.wait("scalar", s_ve)
            ln_stats(t2_scr[:, :], EPS_LN, st8[:, 6:7], st8[:, 7:8])
            v_op("tensor_scalar", t2_scr[:, :], t2_scr[:, :], st8[:, 6:7],
                 st8[:, 7:8], OP.mult, OP.add, waits=W(s_se, s_ve),
                 inc=(s_ve, 1))
            v_op("tensor_tensor", t2_scr[:, :], t2_scr[:, :], buf[:, :],
                 OP.add, inc=(s_ve, 1))
            v_op("engine_nop", inc=(s_ve, 1))
            p.dma("sync", y_out[128 * gi:128 * (gi + 1), :], t2_scr[:, :],
                  s_ld, waits=W(s_ve))

    p.mark_phase("end")
    build_program.phase_marks = list(p.phase_marks)
    tap_outs = []
    for i, tname in enumerate(taps):
        th = tap_handles[tname]
        shp = list(th.ap().shape)
        dt = th.ap().dtype
        out = nc.dram_tensor(f"tap{i}", shp, dt, kind="ExternalOutput").ap()
        for s in (s_ld, s_gld, s_mm, s_ve, s_se, s_ge, s_cc, s_acc):
            p.wait("sync", s)
        p.dma("sync", out, th.ap(), s_ld)
    p.wait("sync", s_ld)
    p.wait("sync", s_acc)
    p.wait("sync", s_gld)
    p.wait("sync", s_cc)

    with nc.Block() as block:
        @block.tensor
        def _(e):
            for f in p.q["tensor"]:
                f(e)

        @block.vector
        def _(e):
            for f in p.q["vector"]:
                f(e)

        @block.scalar
        def _(e):
            for f in p.q["scalar"]:
                f(e)

        @block.gpsimd
        def _(e):
            for f in p.q["gpsimd"]:
                f(e)

        @block.sync
        def _(e):
            for f in p.q["sync"]:
                f(e)

    return nc


def _make_inputs(inputs):
    f32 = np.float32
    bf = ml_dtypes.bfloat16
    x = np.asarray(inputs["L0_attn_x"], f32).reshape(S, DM)
    WqA = np.asarray(inputs["L0_attn_Wq"], f32)
    WkA = np.asarray(inputs["L0_attn_Wk"], f32)
    WvA = np.asarray(inputs["L0_attn_Wv"], f32)
    bqA = np.asarray(inputs["L0_attn_bq"], f32).reshape(1, DM)
    bkA = np.asarray(inputs["L0_attn_bk"], f32).reshape(1, DM)
    bvA = np.asarray(inputs["L0_attn_bv"], f32).reshape(1, DM)
    WoA = np.asarray(inputs["L0_attn_Wo"], f32)
    boA = np.asarray(inputs["L0_attn_bo"], f32).reshape(1, DM)
    ln1 = np.asarray(inputs["L0_ffn_ln1_w"], f32).reshape(1, DM)
    ln2 = np.asarray(inputs["L0_ffn_ln2_w"], f32).reshape(1, DM)
    WqBT = np.ascontiguousarray(np.asarray(inputs["L0_ffn_Wq"], f32).T)
    WkBT = np.ascontiguousarray(np.asarray(inputs["L0_ffn_Wk"], f32).T)
    WvBT = np.ascontiguousarray(np.asarray(inputs["L0_ffn_Wv"], f32).T)
    WoB = np.asarray(inputs["L0_ffn_Wo"], f32)
    A_lora = np.asarray(inputs["L0_ffn_A"], f32)
    B_lora = np.asarray(inputs["L0_ffn_B"], f32)
    W_inT = np.ascontiguousarray(np.asarray(inputs["L0_ffn_W_in_base"], f32).T)
    W_upT = np.ascontiguousarray(np.asarray(inputs["L0_ffn_W_up_base"], f32).T)
    W_downT = np.ascontiguousarray(np.asarray(inputs["L0_ffn_W_down"], f32).T)

    ident = np.eye(128, dtype=bf)
    trilA = np.triu(np.ones((128, 128), np.float32)).astype(bf)
    maskB = np.zeros((4, 128, 512), np.float32)
    triu = np.triu(np.ones((128, 128), np.float32))
    for j in range(4):
        for t in range(4):
            if t > j:
                maskB[j][:, 128 * t:128 * (t + 1)] = 1.0
            elif t == j:
                maskB[j][:, 128 * t:128 * (t + 1)] = triu
    maskB = maskB.astype(bf)
    ones128 = np.ones((128, 128), dtype=bf)
    ones1f = np.ones((1, 128), dtype=f32)

    in_maps = []
    for c in range(NCORES):
        cols = slice(256 * c, 256 * (c + 1))
        in_maps.append({
            "x_in": np.ascontiguousarray(x[T * c:T * (c + 1)]),
            "cid": np.array([[c]], np.int32),
            "ident": ident, "trilA": trilA, "maskB": maskB,
            "ones128": ones128, "ones1x128f": ones1f,
            "WqA": np.ascontiguousarray(WqA[:, cols]),
            "WkA": np.ascontiguousarray(WkA[:, cols]),
            "WvA": np.ascontiguousarray(WvA[:, cols]),
            "bqA": np.ascontiguousarray(bqA[:, cols]),
            "bkA": np.ascontiguousarray(bkA[:, cols]),
            "bvA": np.ascontiguousarray(bvA[:, cols]),
            "WoA": WoA, "boA": boA, "ln1w": ln1, "ln2w": ln2,
            "WqBT": np.ascontiguousarray(WqBT[:, cols]),
            "WkBT": np.ascontiguousarray(WkBT[:, cols]),
            "WvBT": np.ascontiguousarray(WvBT[:, cols]),
            "WoB": WoB, "A_lora": A_lora, "B_lora": B_lora,
            "W_inT": W_inT, "W_upT": W_upT, "W_downT": W_downT,
        })
    return in_maps


def kernel(**inputs):
    in_maps = _make_inputs(inputs)
    nc = build_program()
    res = run_bass_kernel_spmd(nc, in_maps, core_ids=list(range(NCORES)))
    y = np.concatenate([res.results[c]["y"] for c in range(NCORES)], axis=0)
    return y.reshape(1, S, DM).astype(np.float32)
